# version 10
# baseline (speedup 1.0000x reference)
import numpy as np
import ml_dtypes
from contextlib import ExitStack

import concourse.bass as bass
import concourse.mybir as mybir
from concourse.bass_utils import run_bass_kernel_spmd

F32 = mybir.dt.float32
BF16 = mybir.dt.bfloat16
FP8 = mybir.dt.float8e4
I8 = mybir.dt.int8
AF = mybir.ActivationFunctionType
ALU = mybir.AluOpType
AX = mybir.AxisListType

B, S, E = 4, 2048, 2048
H, D = 16, 128
TOK = 1024
NCH = 16
NEG = -3.0e38
CLAMP = -80.0

KB = 1024


class Seg:

    def __init__(self, name, nbytes):
        self.name = name
        self.nbytes = nbytes
        self.free = [(0, nbytes)]
        self.live = {}

    def alloc(self, name, nbytes, align=4):
        for i, (off, ln) in enumerate(self.free):
            aoff = (off + align - 1) // align * align
            if aoff + nbytes <= off + ln:
                pre = (off, aoff - off)
                post = (aoff + nbytes, off + ln - (aoff + nbytes))
                repl = [p for p in (pre, post) if p[1] > 0]
                self.free[i:i + 1] = repl
                self.live[name] = (aoff, nbytes)
                return aoff
        raise MemoryError(
            f"{self.name}: cannot alloc {name} ({nbytes}B/part); "
            f"live={ {k: v for k, v in self.live.items()} }")

    def release(self, name):
        off, ln = self.live.pop(name)
        self.free.append((off, ln))
        self.free.sort()
        merged = []
        for seg in self.free:
            if merged and merged[-1][0] + merged[-1][1] == seg[0]:
                merged[-1] = (merged[-1][0], merged[-1][1] + seg[1])
            else:
                merged.append(list(seg))
        self.free = [tuple(s) for s in merged]


class Prog:
    ENGINES = ("sync", "tensor", "vector", "scalar", "gpsimd")

    def __init__(self, nc):
        self.nc = nc
        self.ops = {e: [] for e in self.ENGINES}

    def emit(self, eng, fn):
        self.ops[eng].append(fn)

    def run(self, block):
        ops = self.ops

        @block.sync
        def _(e):
            for f in ops["sync"]:
                f(e)

        @block.tensor
        def _(e):
            for f in ops["tensor"]:
                f(e)

        @block.vector
        def _(e):
            for f in ops["vector"]:
                f(e)

        @block.scalar
        def _(e):
            for f in ops["scalar"]:
                f(e)

        @block.gpsimd
        def _(e):
            for f in ops["gpsimd"]:
                f(e)


class Sem:

    def __init__(self, handle):
        self.h = handle
        self.n = 0

    def bump(self, k):
        self.n += k
        return self.n


def rearr_lhsT(wT, p=128, j=128):
    K, M = wT.shape
    kc, mt = K // p, M // j
    a = wT.reshape(kc, p, mt, j).transpose(2, 1, 0, 3)
    return np.ascontiguousarray(a.reshape(mt, p, kc * j))


def build_nc(dbg_phase=None):
    nc = bass.Bass("TRN2", target_bir_lowering=False, debug=False)
    dt = nc.dram_tensor

    x0T = dt("x0T", [E, S], F32, kind="ExternalInput").ap()
    x0To = dt("x0To", [E, TOK], F32, kind="ExternalInput").ap()
    wq = dt("wq", [H, 128, E], F32, kind="ExternalInput").ap()
    wkv = dt("wkv", [2, 128, E], F32, kind="ExternalInput").ap()
    wkvg = dt("wkvg", [128, E // 2], F32, kind="ExternalInput").ap()
    wout = dt("wout", [NCH, 128, E], F32, kind="ExternalInput").ap()
    gatew = dt("gatew", [48, 128, E], F32, kind="ExternalInput").ap()
    upw = dt("upw", [48, 128, E], F32, kind="ExternalInput").ap()
    downw2 = dt("downw2", [6, 24, 128, 768], F32, kind="ExternalInput").ap()
    gsc = dt("gsc", [128, 48], F32, kind="ExternalInput").ap()
    usc = dt("usc", [128, 48], F32, kind="ExternalInput").ap()
    dsc = dt("dsc", [128, NCH], F32, kind="ExternalInput").ap()
    qkvw = dt("qkvw", [32, 128, E], F32, kind="ExternalInput").ap()
    oprojw = dt("oprojw", [NCH, 128, E], F32, kind="ExternalInput").ap()
    gammab = dt("gammab", [128, 2 * NCH], F32, kind="ExternalInput").ap()
    wg1 = dt("wg1", [64, 128, E], F32, kind="ExternalInput").ap()
    wu1 = dt("wu1", [64, 128, E], F32, kind="ExternalInput").ap()
    wd2 = dt("wd2", [6, 32, 128, 768], F32, kind="ExternalInput").ap()
    vweight = dt("vweight", [NCH, 128, E], F32, kind="ExternalInput").ap()
    cosK = dt("cosK", [128, S], F32, kind="ExternalInput").ap()
    sinK = dt("sinK", [128, S], F32, kind="ExternalInput").ap()
    cosQ = dt("cosQ", [128, TOK], F32, kind="ExternalInput").ap()
    sinQ = dt("sinQ", [128, TOK], F32, kind="ExternalInput").ap()
    maskb = dt("maskb", [128, 1792], BF16, kind="ExternalInput").ap()
    identb = dt("identb", [128, 128], BF16, kind="ExternalInput").ap()
    identf = dt("identf", [128, 128], F32, kind="ExternalInput").ap()
    onesb = dt("onesb", [128, 1], BF16, kind="ExternalInput").ap()
    onesf = dt("onesf", [1, 128], F32, kind="ExternalInput").ap()
    consts = dt("consts", [128, 8], F32, kind="ExternalInput").ap()

    out = dt("out", [E, TOK], F32, kind="ExternalOutput").ap()

    fuse0 = dt("fuse0", [48 * 128, TOK], BF16, kind="Internal").ap()
    fuse1 = dt("fuse1", [64 * 128, TOK], BF16, kind="Internal").ap()
    qspill = dt("qspill", [NCH * 128, TOK], BF16, kind="Internal").ap()
    ccKin = dt("ccKin", [E, TOK], BF16, kind="Internal").ap()
    qkvw_b = dt("qkvw_b", [32, 128, E], BF16, kind="Internal").ap()
    vw_b = dt("vw_b", [NCH, 128, E], BF16, kind="Internal").ap()
    ccKout = dt("ccKout", [2 * E, TOK], BF16, kind="Internal").ap()
    ccVin = dt("ccVin", [TOK, E], BF16, kind="Internal").ap()
    ccVout = dt("ccVout", [2 * TOK, E], BF16, kind="Internal").ap()
    RG = [[0, 1], [2, 3], [4, 5], [6, 7]]

    ctx = ExitStack()
    e = ctx.enter_context
    AF32 = e(nc.sbuf_tensor("AF32", [128, 84 * KB // 4], F32))
    ABF = e(nc.sbuf_tensor("ABF", [128, 96 * KB // 2], BF16))
    AFP8 = e(nc.sbuf_tensor("AFP8", [128, 4 * KB], FP8))
    AI8 = e(nc.sbuf_tensor("AI8", [128, 2 * KB], I8))
    CST = e(nc.sbuf_tensor("CST", [128, 256], F32))
    CSTB = e(nc.sbuf_tensor("CSTB", [128, 2048], BF16))
    CSTF = e(nc.sbuf_tensor("CSTF", [128, 1024 // 4], F32))

    segf = Seg("f32", 84 * KB)
    segb = Seg("bf16", 96 * KB)

    def f32(name, cols):
        off = segf.alloc(name, cols * 4)
        return AF32[:, off // 4: off // 4 + cols]

    def bf(name, cols):
        off = segb.alloc(name, cols * 2)
        return ABF[:, off // 2: off // 2 + cols]

    psS = e(nc.psum_tensor("psS", [128, 2048], F32))
    psA = e(nc.psum_tensor("psA", [128, 1024], F32))
    psX6 = e(nc.psum_tensor("psX6", [128, 512], F32))
    psO = e(nc.psum_tensor("psO", [128, 512], F32))

    sT = Sem(e(nc.semaphore("sT")))
    sV = Sem(e(nc.semaphore("sV")))
    sS = Sem(e(nc.semaphore("sS")))
    sG = Sem(e(nc.semaphore("sG")))
    def dsem(name):
        return Sem(e(nc.semaphore(name)))
    sCst = dsem("sCst")
    sKtab = dsem("sKtab")
    sGrel = dsem("sGrel")
    sXb = [dsem("sXb0"), dsem("sXb1")]
    sWkv = dsem("sWkv")
    sRb = dsem("sRb")
    sWq = dsem("sWq")
    sPc = dsem("sPc")
    sSwap = dsem("sSwap")
    sRes = [dsem("sRes0"), dsem("sRes1")]
    sWo = [dsem("sWo0"), dsem("sWo1")]
    sO = dsem("sO")

    p = Prog(nc)
    E_ = p.emit

    def dma(eng_name, dst, src, sem):
        tok = sem.bump(16)
        E_(eng_name, lambda E, d=dst, s=src, h=sem.h: E.dma_start(out=d, in_=s)
           .then_inc(h, 16))
        return tok

    def wait(eng_name, sem, val):
        E_(eng_name, lambda E, h=sem.h, v=val: E.wait_ge(h, v))

    def mark(eng_name, instr_fn, sem, k=1):
        if eng_name in ("vector", "scalar", "gpsimd") and sem.n > 0:
            E_(eng_name, lambda E, h=sem.h, v=sem.n: E.wait_ge(h, v))
        tok = sem.bump(k)
        E_(eng_name, lambda E, f=instr_fn, h=sem.h, kk=k: f(E).then_inc(h, kk))
        return tok

    def emit(eng_name, instr_fn):
        E_(eng_name, instr_fn)

    c_cosQ = f32("cosQ", TOK)
    c_sinQ = f32("sinQ", TOK)
    t_const = 0
    t_const = dma("sync", CST[:, 0:8], consts, sCst)
    t_const = dma("sync", CST[:, 8:56], gsc, sCst)
    t_const = dma("sync", CST[:, 56:104], usc, sCst)
    t_const = dma("sync", CST[:, 104:120], dsc, sCst)
    gammaap = CST[:, 120:136]
    betaap = CST[:, 136:152]
    t_const = dma("sync", CST[:, 120:152], gammab, sCst)
    t_const = dma("sync", CSTB[:, 0:128], identb, sCst)
    t_const = dma("sync", CSTB[:, 128:129], onesb, sCst)
    t_const = dma("sync", CSTB[:, 256:2048], maskb, sCst)
    t_const = dma("sync", CSTF[:, 0:128], identf, sCst)
    t_const = dma("sync", CSTF[0:1, 128:256], onesf, sCst)
    t_const = dma("sync", c_cosQ, cosQ, sCst)
    t_const = dma("sync", c_sinQ, sinQ, sCst)
    CONST_TOK = t_const
    identb_ap = CSTB[:, 0:128]
    onesb_ap = CSTB[:, 128:129]
    maskb256 = CSTB[:, 256:512]
    maskb512 = CSTB[:, 512:1024]
    maskE = CSTB[:, 1024:1536]
    maskO = CSTB[:, 1536:2048]
    identf_ap = CSTF[:, 0:128]
    onesf_ap = CSTF[0:1, 128:256]
    c0 = lambda j: CST[:, j:j + 1]
    c0r = lambda j: CST[0:1, j:j + 1]
    gsc_col = lambda m: CST[:, 8 + m:9 + m]
    usc_col = lambda m: CST[:, 56 + m:57 + m]
    dsc_col = lambda m: CST[:, 104 + m:105 + m]

    for en in ("tensor", "vector", "scalar", "gpsimd"):
        wait(en, sCst, CONST_TOK)

    R = [None] * NCH
    Rb = [None] * NCH

    def alloc_R():
        for c in range(NCH):
            R[c] = f32(f"R{c}", TOK)

    def alloc_Rb():
        for c in range(NCH):
            Rb[c] = bf(f"Rb{c}", TOK)

    def post_ln(eps_col, src_chunks, dst_chunks, dstb_chunks=None,
                gamma_col=None, beta_col=None):
        tb = [bf("ln_tb", TOK), bf("ln_tb2", TOK)]
        sq = [bf("ln_sq", TOK), bf("ln_sq2", TOK)]
        cast_toks = []
        mm_tok = None
        for c in range(NCH):
            tcast = mark("scalar", lambda E, s=src_chunks[c], d=tb[c % 2]:
                         E.copy(out=d, in_=s), sS)
            tsq = mark("scalar", lambda E, s=src_chunks[c], d=sq[c % 2]:
                       E.activation(out=d, in_=s, func=AF.Square), sS)
            wait("tensor", sS, tsq)
            for n in range(2):
                last = (c == NCH - 1)
                mm_tok = mark("tensor", lambda E, d=psS[0:1, 0:0], c_=c, n_=n,
                              t=tb[c % 2], last_=last:
                              E.matmul(out=psS[0:1, n_ * 512:(n_ + 1) * 512],
                                       lhsT=onesb_ap,
                                       rhs=t[:, n_ * 512:(n_ + 1) * 512],
                                       start=(c_ == 0), stop=last_), sT)
                mm_tok = mark("tensor", lambda E, c_=c, n_=n, q=sq[c % 2],
                              last_=last:
                              E.matmul(out=psS[0:1, 1024 + n_ * 512:1024 + (n_ + 1) * 512],
                                       lhsT=onesb_ap,
                                       rhs=q[:, n_ * 512:(n_ + 1) * 512],
                                       start=(c_ == 0), stop=last_), sT)
            if c >= 1:
                wait("scalar", sT, mm_tok)
        meanB = f32("ln_meanB", TOK)
        invB = f32("ln_invB", TOK)
        mean_r = meanB[0:1, 0:TOK]
        var_r = invB[0:1, 0:TOK]
        inv_r = var_r
        wait("scalar", sT, mm_tok)
        t1 = mark("scalar", lambda E: E.mul(out=mean_r, in_=psS[0:1, 0:1024],
                                            mul=c0r(0)), sS)
        t1 = mark("scalar", lambda E: E.activation(out=var_r, in_=mean_r,
                                                   func=AF.Square), sS)
        wait("vector", sS, t1)
        t2 = mark("vector", lambda E: E.scalar_tensor_tensor(
            out=var_r, in0=psS[0:1, 1024:2048], scalar=c0r(0), in1=var_r,
            op0=ALU.mult, op1=ALU.subtract), sV)
        wait("scalar", sV, t2)
        t3 = mark("scalar", lambda E: E.activation(out=inv_r, in_=var_r,
                                                   func=AF.Ln, bias=eps_col), sS)
        t3 = mark("scalar", lambda E: E.activation(out=inv_r, in_=inv_r,
                                                   func=AF.Exp, scale=c0r(3)), sS)
        wait("tensor", sS, t3)
        tb_tok = None
        for n in range(2):
            tb_tok = mark("tensor", lambda E, n_=n: E.matmul(
                out=psS[:, n_ * 512:(n_ + 1) * 512], lhsT=onesf_ap,
                rhs=mean_r[0:1, n_ * 512:(n_ + 1) * 512],
                start=True, stop=True), sT)
            tb_tok = mark("tensor", lambda E, n_=n: E.matmul(
                out=psS[:, 1024 + n_ * 512:1024 + (n_ + 1) * 512], lhsT=onesf_ap,
                rhs=inv_r[0:1, n_ * 512:(n_ + 1) * 512],
                start=True, stop=True), sT)
        wait("scalar", sT, tb_tok)
        t4 = mark("scalar", lambda E: E.copy(out=meanB, in_=psS[:, 0:1024]), sS)
        t4 = mark("scalar", lambda E: E.copy(out=invB, in_=psS[:, 1024:2048]), sS)
        scratch = f32("ln_scr", TOK)
        last_v = None
        last_s = None
        for c in range(NCH):
            wait("vector", sS, t4)
            if last_s is not None:
                wait("vector", sS, last_s)
            last_v = mark("vector", lambda E, s=src_chunks[c]:
                          E.tensor_sub(out=scratch, in0=s, in1=meanB), sV)
            if gamma_col is None:
                last_v = mark("vector", lambda E, d=dst_chunks[c]:
                              E.tensor_mul(out=d, in0=scratch, in1=invB), sV)
                if dstb_chunks is not None:
                    wait("scalar", sV, last_v)
                    last_s = mark("scalar", lambda E, d=dstb_chunks[c],
                                  s=dst_chunks[c]: E.copy(out=d, in_=s), sS)
            else:
                last_v = mark("vector", lambda E:
                              E.tensor_mul(out=scratch, in0=scratch, in1=invB), sV)
                wait("scalar", sV, last_v)
                last_s = mark("scalar", lambda E, d=dstb_chunks[c],
                              g=gamma_col(c), bcol=beta_col(c):
                              E.activation(out=d, in_=scratch, func=AF.Identity,
                                           bias=bcol, scale=g), sS)
        for nm in ("ln_tb", "ln_tb2", "ln_sq", "ln_sq2"):
            segb.release(nm)
        for nm in ("ln_meanB", "ln_invB", "ln_scr"):
            segf.release(nm)
        return last_v, last_s

    kp_feat = f32("kp_feat", S)
    vp_feat = f32("vp_feat", S)
    kvg_sb = f32("kvg_sb", S)
    xb = [bf("xb0", S), bf("xb1", S)]
    wkv_b = bf("wkv_b", NCH * 256)
    wkvg_b = bf("wkvg_b", NCH * 64)
    t_wkv0 = dma("gpsimd", wkv_b[:, 0:E], wkv[0], sWkv)
    t_wkv1 = dma("gpsimd", wkv_b[:, E:2 * E], wkv[1], sWkv)
    t_wkvg = dma("gpsimd", wkvg_b, wkvg, sWkv)

    def xstream_pass(tag, lhsT_w, out_psum_slices, prev_gate=None):
        if prev_gate is not None:
            wait("gpsimd", sT, prev_gate)
        toks = []
        for c in range(NCH):
            if c >= 2:
                wait("gpsimd", sT, toks[c - 2])
            tok_x = dma("gpsimd", xb[c % 2], x0T[c * 128:(c + 1) * 128, :], sXb[c % 2])
            wait("tensor", sXb[c % 2], tok_x)
            for n in range(4):
                tok = mark("tensor", lambda E, c_=c, n_=n: E.matmul(
                    out=out_psum_slices[n_], lhsT=lhsT_w(c_),
                    rhs=xb[c_ % 2][:, n_ * 512:(n_ + 1) * 512],
                    start=(c_ == 0), stop=(c_ == NCH - 1)), sT)
            toks.append(tok)
        return toks[-1]

    wait("tensor", sWkv, t_wkvg)
    tok_kp = xstream_pass(
        "kp", lambda c: wkv_b[:, c * 128:(c + 1) * 128],
        [psS[:, n * 512:(n + 1) * 512] for n in range(4)])
    wait("scalar", sT, tok_kp)
    tkp_ev = mark("scalar", lambda E: E.copy(out=kp_feat, in_=psS[:, 0:2048]), sS)

    wait("tensor", sS, tkp_ev)
    tok_vp = xstream_pass(
        "vp", lambda c: wkv_b[:, E + c * 128:E + (c + 1) * 128],
        [psS[:, n * 512:(n + 1) * 512] for n in range(4)], prev_gate=tok_kp)
    wait("scalar", sT, tok_vp)
    tvp_ev = mark("scalar", lambda E: E.copy(out=vp_feat, in_=psS[:, 0:2048]), sS)

    wait("tensor", sS, tvp_ev)
    wait("tensor", sWkv, t_wkvg)
    tok_kvg = xstream_pass(
        "kvg", lambda c: wkvg_b[:, c * 64:(c + 1) * 64],
        [psS[0:64, n * 512:(n + 1) * 512] for n in range(4)], prev_gate=tok_vp)
    wait("scalar", sT, tok_kvg)
    tkvg_ev = mark("scalar", lambda E: E.copy(out=kvg_sb[0:64, :],
                                              in_=psS[0:64, 0:2048]), sS)
    segb.release("xb0")
    segb.release("xb1")

    c_cosK = f32("cosK", S)
    c_sinK = f32("sinK", S)
    tck = dma("sync", c_cosK, cosK, sKtab)
    tck = dma("sync", c_sinK, sinK, sKtab)
    kswap = f32("kswap", S)
    wait("sync", sS, tkp_ev)
    tsw = dma("sync", kswap[0:64, :], kp_feat[64:128, :], sSwap)
    tsw = dma("sync", kswap[64:128, :], kp_feat[0:64, :], sSwap)
    kp_rot = f32("kp_rot", S)
    wait("vector", sSwap, tsw)
    wait("vector", sKtab, tck)
    tv = mark("vector", lambda E: E.tensor_mul(out=kp_rot, in0=kp_feat, in1=c_cosK), sV)
    tv = mark("vector", lambda E: E.tensor_mul(out=kswap, in0=kswap, in1=c_sinK), sV)
    tv = mark("vector", lambda E: E.tensor_add(out=kp_rot, in0=kp_rot, in1=kswap), sV)
    segf.release("cosK")
    segf.release("sinK")
    segf.release("kp_feat")
    segf.release("kswap")

    kp_rot_tok = f32("kp_rot_tok", S)
    vp_tok = f32("vp_tok", S)
    tr_tok = None
    for (src, dst, gate_sem, gate_tok) in (
            (kp_rot, kp_rot_tok, sV, tv), (vp_feat, vp_tok, sS, tvp_ev)):
        for g in range(4):
            wait("tensor", gate_sem, gate_tok)
            for j in range(4):
                t_i = g * 4 + j
                tr_tok = mark("tensor", lambda E, s=src, t=t_i, j_=j: E.transpose(
                    out=psA[:, j_ * 128:(j_ + 1) * 128],
                    in_=s[:, t * 128:(t + 1) * 128], identity=identf_ap), sT)
            wait("scalar", sT, tr_tok)
            tr_ev = mark("scalar", lambda E, d=dst, g_=g: E.copy(
                out=d[:, g_ * 512:(g_ + 1) * 512], in_=psA[:, 0:512]), sS)
            wait("tensor", sS, tr_ev)
    segf.release("kp_rot")
    segf.release("vp_feat")

    pipe = f32("pipe", 2 * S)
    g_lo = pipe[0:32, 0:S]
    Cx = pipe[0:32, S:2 * S]
    kvh_rows = kvg_sb[0:32, :]
    wait("sync", sS, tkvg_ev)
    wait("sync", sV, tv)
    t_grel = dma("sync", g_lo, kvg_sb[32:64, :], sGrel)
    wait("scalar", sGrel, t_grel)
    ts_ = mark("scalar", lambda E: E.activation(out=g_lo, in_=g_lo, func=AF.Sigmoid), sS)
    ts_ = mark("scalar", lambda E: E.activation(out=g_lo, in_=g_lo, func=AF.Ln), sS)
    wait("vector", sS, ts_)
    tv_ = mark("vector", lambda E: E.tensor_tensor_scan(
        out=Cx, data0=g_lo, data1=g_lo, initial=0.0, op0=ALU.add, op1=ALU.bypass), sV)
    tv_ = mark("vector", lambda E: E.tensor_sub(out=Cx, in0=Cx, in1=g_lo), sV)
    wait("scalar", sV, tv_)
    ts_ = mark("scalar", lambda E: E.activation(out=Cx, in_=Cx, func=AF.Exp), sS)
    wait("vector", sS, ts_)
    tv_ = mark("vector", lambda E: E.tensor_mul(out=g_lo, in0=kvh_rows, in1=Cx), sV)
    tv_ = mark("vector", lambda E: E.tensor_tensor_scan(
        out=kvh_rows, data0=g_lo, data1=g_lo, initial=0.0,
        op0=ALU.add, op1=ALU.bypass), sV)
    kvh_s = kvh_rows
    khvh_tok = f32("khvh_tok", NCH * 32)
    tre = None
    for g in range(4):
        wait("tensor", sV, tv_)
        if tre is not None:
            wait("tensor", sS, tre)
        trt = None
        for j in range(4):
            t_i = g * 4 + j
            trt = mark("tensor", lambda E, t=t_i, j_=j: E.transpose(
                out=psA[0:128, j_ * 128:j_ * 128 + 32],
                in_=kvh_s[:, t * 128:(t + 1) * 128],
                identity=identf_ap[0:32, 0:32]), sT)
        wait("scalar", sT, trt)
        for j in range(4):
            t_i = g * 4 + j
            tre = mark("scalar", lambda E, t=t_i, j_=j: E.copy(
                out=khvh_tok[:, t * 32:(t + 1) * 32],
                in_=psA[:, j_ * 128:j_ * 128 + 32]), sS)
    wait("tensor", sS, tre)
    segf.release("pipe")
    segf.release("kvg_sb")

    kh_col = lambda tile, h: khvh_tok[:, tile * 32 + h:tile * 32 + h + 1]
    vh_col = lambda tile, h: khvh_tok[:, tile * 32 + H + h:tile * 32 + H + h + 1]

    alloc_Rb()
    wait("gpsimd", sT, tok_kvg)
    t_rb = None
    for c in range(NCH):
        t_rb = dma("gpsimd", Rb[c], x0To[c * 128:(c + 1) * 128, :], sRb)
    RB_TOK = t_rb
    segb.release("wkv_b")
    segb.release("wkvg_b")

    O_feat = bf("O_feat", NCH * TOK)
    PT_buf = bf("PT_buf", NCH * 256)
    Kp_f = bf("Kpf", S)
    Vp_t = bf("Vpt", S)
    Kp_t = bf("Kpt", S)
    qfeat = f32("qfeat", TOK)
    qswap = f32("qswap", TOK)
    qrot = bf("qrot", TOK)
    Pf = bf("Pf", S)
    cols = f32("cols", 16)
    negmax = cols[:, 0:2]
    mh = cols[:, 2:4]
    rowsum = cols[:, 4:6]
    rinv = cols[:, 6:8]
    diagb = bf("diag", 256)
    wqb = [bf("wqb0", NCH * 128)]

    wq_cons = {}
    psX_free = None
    psO_free = None
    psA_free = None
    ofeat_done = [None] * H
    prev_head_t_done = None
    prev_tsh = None
    prev_tex = None
    prev_pt_last = None
    prev_grp_pv = None

    wait("gpsimd", sCst, CONST_TOK)
    for h in range(H):
        if h >= 1:
            wait("gpsimd", sT, wq_cons[h - 1])
        t_wq = dma("gpsimd", wqb[0], wq[h], sWq)
        for j_ in range(3 * h, 3 * h + 3):
            if j_ < 32:
                t_pc = dma("gpsimd", qkvw_b[j_], qkvw[j_], sPc)
            else:
                t_pc = dma("gpsimd", vw_b[j_ - 32], vweight[j_ - 32], sPc)
        wait("tensor", sWq, t_wq)
        wait("tensor", sRb, RB_TOK)
        if psA_free is not None:
            wait("tensor", sS, psA_free)
        if psX_free is not None:
            wait("tensor", sS, psX_free)
        tq = None
        for c in range(NCH):
            for n in range(2):
                tq = mark("tensor", lambda E, c_=c, n_=n: E.matmul(
                    out=psA[:, n_ * 512:(n_ + 1) * 512],
                    lhsT=wqb[0][:, c_ * 128:(c_ + 1) * 128],
                    rhs=Rb[c_][:, n_ * 512:(n_ + 1) * 512],
                    start=(c_ == 0), stop=(c_ == NCH - 1)), sT)
        wq_cons[h] = tq
        wait("scalar", sT, tq)
        if h > 0:
            wait("scalar", sV, head_tqv)
        tqe = mark("scalar", lambda E: E.copy(out=qfeat, in_=psA[:]), sS)
        psA_free = tqe
        if prev_head_t_done is not None:
            wait("vector", sT, prev_head_t_done)
        if h == 0:
            wait("vector", sS, tre)
        tkv = None
        for t_i in range(NCH):
            tkv = mark("vector", lambda E, t=t_i, hh=h: E.tensor_scalar(
                out=Kp_t[:, t * 128:(t + 1) * 128],
                in0=kp_rot_tok[:, t * 128:(t + 1) * 128],
                scalar1=kh_col(t, hh), scalar2=None, op0=ALU.mult), sV)
            tkv = mark("vector", lambda E, t=t_i, hh=h: E.tensor_scalar(
                out=Vp_t[:, t * 128:(t + 1) * 128],
                in0=vp_tok[:, t * 128:(t + 1) * 128],
                scalar1=vh_col(t, hh), scalar2=None, op0=ALU.mult), sV)
        wait("sync", sS, tqe)
        if h > 0:
            wait("sync", sV, head_tqv)
        tqs = dma("sync", qswap[0:64, :], qfeat[64:128, :], sSwap)
        tqs = dma("sync", qswap[64:128, :], qfeat[0:64, :], sSwap)
        wait("vector", sSwap, tqs)
        tqv = mark("vector", lambda E: E.tensor_mul(out=qfeat, in0=qfeat, in1=c_cosQ), sV)
        tqv = mark("vector", lambda E: E.tensor_mul(out=qswap, in0=qswap, in1=c_sinQ), sV)
        tqv = mark("vector", lambda E: E.tensor_add(out=qrot, in0=qfeat, in1=qswap), sV)
        head_tqv = tqv
        wait("tensor", sS, tqe)
        for g in range(2):
            wait("tensor", sV, tkv)
            if psX_free is not None:
                wait("tensor", sS, psX_free)
            trt = None
            for j in range(8):
                t_i = g * 8 + j
                trt = mark("tensor", lambda E, t=t_i, j_=j: E.matmul(
                    out=psA[:, j_ * 128:(j_ + 1) * 128],
                    lhsT=Kp_t[:, t * 128:(t + 1) * 128], rhs=identb_ap,
                    start=True, stop=True), sT)
            wait("scalar", sT, trt)
            psX_free = mark("scalar", lambda E, g_=g: E.copy(
                out=Kp_f[:, g_ * 1024:(g_ + 1) * 1024], in_=psA[:]), sS)
        tKpf = psX_free

        def slices_of(keys):
            out, off = [], 0
            while off < keys:
                w = min(512, keys - off)
                out.append((off, w))
                off += w
            return out

        def emit_pt(qt_, diag_tok):
            nonlocal psX_free, prev_pt_last
            keys = 256 * (qt_ + 1)
            ntile = keys // 128
            dg = diagb[:, (qt_ % 2) * 128:(qt_ % 2) * 128 + 128]
            src_p = Pfp[qt_ % 2]
            trt = None
            for g0 in range(0, ntile, 8):
                gn = min(8, ntile - g0)
                wait("tensor", sV, diag_tok)
                if psX_free is not None:
                    wait("tensor", sS, psX_free)
                for j in range(gn):
                    kc = g0 + j
                    trt = mark("tensor", lambda E, k_=kc, j_=j, d=dg, sp=src_p:
                               E.matmul(
                        out=psA[:, j_ * 128:(j_ + 1) * 128],
                        lhsT=sp[:, k_ * 128:(k_ + 1) * 128], rhs=d,
                        start=True, stop=True), sT)
                wait("scalar", sT, trt)
                dst = PT_buf[:, (qt_ % 2) * 2048 + g0 * 128:
                             (qt_ % 2) * 2048 + (g0 + gn) * 128]
                psX_free = mark("scalar", lambda E, d=dst, g_=gn: E.copy(
                    out=d, in_=psA[:, 0:g_ * 128]), sS)
            prev_pt_last = trt

        def emit_pv(qt_, hh):
            nonlocal psO_free, prev_grp_pv
            keys = 256 * (qt_ + 1)
            ntile = keys // 128
            wait("tensor", sS, psX_free)
            if qt_ % 4 == 0 and psO_free is not None:
                wait("tensor", sS, psO_free)
            tpv = None
            for kc in range(ntile):
                tpv = mark("tensor", lambda E, k_=kc, q_=qt_: E.matmul(
                    out=psO[:, (q_ % 4) * 128:(q_ % 4) * 128 + 128],
                    lhsT=Vp_t[:, k_ * 128:(k_ + 1) * 128],
                    rhs=PT_buf[:, (q_ % 2) * 2048 + k_ * 128:
                               (q_ % 2) * 2048 + (k_ + 1) * 128],
                    start=(k_ == 0), stop=(k_ == ntile - 1)), sT)
            prev_grp_pv = tpv
            if qt_ % 4 == 3:
                wait("scalar", sT, tpv)
                psO_free = mark("scalar", lambda E, hh_=hh, g_=qt_ // 4: E.copy(
                    out=O_feat[:, hh_ * 1024 + g_ * 512:hh_ * 1024 + (g_ + 1) * 512],
                    in_=psO[:]), sS)

        Pfp = [Pf, Kp_t]
        pend_pt = None
        pend_exp = None
        shift_toks = None
        for qt in range(8):
            keys = 256 * (qt + 1)
            sls = slices_of(keys)
            wait("tensor", sV, tqv)
            wait("tensor", sS, tKpf)
            tqk = None
            for i, (off, w) in enumerate(sls):
                if shift_toks is not None and i < len(shift_toks):
                    wait("tensor", sS, shift_toks[min(i, len(shift_toks) - 1)])
                last = (i == len(sls) - 1)
                tqk = mark("tensor", lambda E, o_=off, w_=w, q_=qt,
                           sp_=not last: E.matmul(
                    out=psS[:, o_:o_ + w_],
                    lhsT=qrot[:, q_ * 128:(q_ + 1) * 128],
                    rhs=Kp_f[:, o_:o_ + w_],
                    start=True, stop=sp_), sT)
                if last:
                    m_ap = maskb256 if w == 256 else maskb512
                    tqk = mark("tensor", lambda E, o_=off, w_=w, m=m_ap: E.matmul(
                        out=psS[:, o_:o_ + w_], lhsT=identb_ap, rhs=m,
                        start=False, stop=True), sT)
            if pend_pt is not None:
                emit_pt(*pend_pt)
                emit_pv(pend_pt[0], h)
                pend_pt = None
            pm = qt % 2
            nm = negmax[:, pm:pm + 1]
            wait("vector", sT, tqk)
            if shift_toks is not None:
                wait("vector", sS, shift_toks[-1])
            tcmb = mark("vector", lambda E, k_=keys, d=nm: E.reduce_max(
                out=d, in_=psS[:, 0:k_], axis=AX.X, negate=True), sV)
            if pend_exp is not None:
                q_, tex_ = pend_exp
                wait("vector", sS, tex_)
                mark("vector", lambda E, q2=q_ % 2: E.reciprocal(
                    out=rinv[:, q2:q2 + 1], in_=rowsum[:, q2:q2 + 1]), sV)
                if prev_pt_last is not None:
                    wait("vector", sT, prev_pt_last)
                tdg = mark("vector", lambda E, q2=q_ % 2: E.tensor_scalar(
                    out=diagb[:, q2 * 128:(q2 + 1) * 128], in0=identb_ap,
                    scalar1=rinv[:, q2:q2 + 1], scalar2=None, op0=ALU.mult), sV)
                pend_pt = (q_, tdg)
                pend_exp = None
            wait("scalar", sV, tcmb)
            if prev_pt_last is not None:
                wait("scalar", sT, prev_pt_last)
            shift_toks = []
            for (off, w) in sls:
                tshn = mark("scalar", lambda E, o_=off, w_=w, b=nm, q2=pm:
                            E.activation(
                    out=Pfp[q2][:, o_:o_ + w_], in_=psS[:, o_:o_ + w_],
                    func=AF.Identity, bias=b), sS)
                shift_toks.append(tshn)
            prev_tsh = shift_toks[-1]
            tex = mark("scalar", lambda E, q2=pm, k_=keys: E.activation(
                out=Pfp[q2][:, 0:k_], in_=Pfp[q2][:, 0:k_], func=AF.Exp,
                accum_out=rowsum[:, q2:q2 + 1]), sS)
            pend_exp = (qt, tex)
        if pend_pt is not None:
            emit_pt(*pend_pt)
            emit_pv(pend_pt[0], h)
        q_, tex_ = pend_exp
        wait("vector", sS, tex_)
        mark("vector", lambda E, q2=q_ % 2: E.reciprocal(
            out=rinv[:, q2:q2 + 1], in_=rowsum[:, q2:q2 + 1]), sV)
        if prev_pt_last is not None:
            wait("vector", sT, prev_pt_last)
        tdg = mark("vector", lambda E, q2=q_ % 2: E.tensor_scalar(
            out=diagb[:, q2 * 128:(q2 + 1) * 128], in0=identb_ap,
            scalar1=rinv[:, q2:q2 + 1], scalar2=None, op0=ALU.mult), sV)
        emit_pt(q_, tdg)
        emit_pv(q_, h)
        pend_exp = None
        ofeat_done[h] = psO_free
        prev_head_t_done = prev_grp_pv

    segf.release("qfeat")
    segf.release("qswap")

    segb.release("Pf")
    segf.release("kp_rot_tok")
    segf.release("vp_tok")
    segf.release("khvh_tok")
    segb.release("Kpf")
    segb.release("Vpt")
    segb.release("Kpt")
    segb.release("qrot")
    segb.release("PT_buf")
    segf.release("cosQ")
    segf.release("sinQ")
    segf.release("cols")
    alloc_R()
    woutb = [bf("wob0", NCH * 128), bf("wob1", NCH * 128)]
    rres = [f32("rres0", TOK), f32("rres1", TOK)]
    wout_cons = {}
    rres_cons = {}
    wait("tensor", sS, ofeat_done[H - 1])
    wait("gpsimd", sT, prev_head_t_done)
    wait("sync", sT, prev_head_t_done)
    ev_tok = None
    for mt in range(NCH):
        if mt >= 2:
            wait("gpsimd", sT, wout_cons[mt - 2])
            wait("sync", sV, rres_cons[mt - 2])
        t_w = dma("gpsimd", woutb[mt % 2], wout[mt], sWo[mt % 2])
        t_r = dma("sync", rres[mt % 2], x0To[mt * 128:(mt + 1) * 128, :], sRes[mt % 2])
        wait("tensor", sWo[mt % 2], t_w)
        if psA_free is not None:
            wait("tensor", sV if mt > 0 else sS, psA_free)
        tmm = None
        for kc in range(NCH):
            for n in range(2):
                tmm = mark("tensor", lambda E, k_=kc, n_=n, m_=mt: E.matmul(
                    out=psA[:, n_ * 512:(n_ + 1) * 512],
                    lhsT=woutb[m_ % 2][:, k_ * 128:(k_ + 1) * 128],
                    rhs=O_feat[:, k_ * 1024 + n_ * 512:k_ * 1024 + (n_ + 1) * 512],
                    start=(k_ == 0), stop=(k_ == NCH - 1)), sT)
        wout_cons[mt] = tmm
        wait("vector", sT, tmm)
        wait("vector", sRes[mt % 2], t_r)
        ev_tok = mark("vector", lambda E, m_=mt: E.scalar_tensor_tensor(
            out=R[m_], in0=psA[:], scalar=1.0, in1=rres[m_ % 2],
            op0=ALU.mult, op1=ALU.add), sV)
        rres_cons[mt] = ev_tok
        psA_free = ev_tok
    segb.release("O_feat")
    segb.release("wob0")
    segb.release("wob1")
    segf.release("rres0")
    segf.release("rres1")
    segb.release("wqb0")
    wait("scalar", sV, ev_tok)
    lnv, lns = post_ln(c0r(1), R, R, dstb_chunks=Rb)

    def finish():
        tdone = None
        wait("sync", sS, sS.n)
        wait("sync", sV, sV.n)
        for c in range(NCH):
            tdone = dma("sync", out[c * 128:(c + 1) * 128, :], R[c], sO)
        wait("sync", sO, tdone)
        p_finalize(nc, p, ctx)
        return nc

    if dbg_phase == "A":
        return finish()

    sWst = dsem("sWst")
    sDst = dsem("sDst")
    sFz = [dsem("sFz0"), dsem("sFz1")]
    sFr = [dsem("sFr0"), dsem("sFr1")]
    sDw = [dsem("sDw0"), dsem("sDw1")]

    wstage = f32("wstage", S)
    scr1 = f32("b_scr1", TOK)
    scr2 = f32("b_scr2", TOK)
    w8a = AFP8[:, 0:2048]
    w8b = AFP8[:, 2048:4096]
    gb = [bf("gb0", S), bf("gb1", S)]
    ub = [bf("ub0", S), bf("ub1", S)]
    fzb = [bf("fz0", TOK), bf("fz1", TOK)]

    wait("gpsimd", sS, lns)
    g_cons = {}
    u_cons = {}
    fz_cons = {}
    prev_silu = None
    prev_ucp = None
    prev_v8 = {}
    prev_s8 = {}
    prev_tf = None
    silu_par = [None, None]
    ucp_par = [None, None]
    gtgt = [[psA[:, 0:512], psA[:, 512:1024]], [psX6[:], psO[:]]]
    utgt = [psS[:, 0:1024], psS[:, 1024:2048]]

    def stage_pair(m):
        if m >= 1:
            wait("gpsimd", sV, prev_v8[(m - 1, "u")])
        t_g = dma("gpsimd", wstage, gatew[m], sWst)
        wait("vector", sWst, t_g)
        if m >= 1:
            wait("vector", sS, prev_s8[(m - 1, "g")])
        v8g = mark("vector", lambda E: E.tensor_copy(out=w8a, in_=wstage), sV)
        prev_v8[(m, "g")] = v8g
        if m >= 2 and (m - 2) in g_cons:
            wait("scalar", sT, g_cons[m - 2])
        wait("scalar", sV, v8g)
        s8g = mark("scalar", lambda E, m_=m: E.copy(out=gb[m_ % 2], in_=w8a), sS)
        prev_s8[(m, "g")] = s8g
        wait("gpsimd", sV, v8g)
        t_u = dma("gpsimd", wstage, upw[m], sWst)
        wait("vector", sWst, t_u)
        if m >= 1:
            wait("vector", sS, prev_s8[(m - 1, "u")])
        v8u = mark("vector", lambda E: E.tensor_copy(out=w8b, in_=wstage), sV)
        prev_v8[(m, "u")] = v8u
        if m >= 2 and (m - 2) in u_cons:
            wait("scalar", sT, u_cons[m - 2])
        wait("scalar", sV, v8u)
        s8u = mark("scalar", lambda E, m_=m: E.copy(out=ub[m_ % 2], in_=w8b), sS)
        prev_s8[(m, "u")] = s8u
        return s8g, s8u

    staged = {0: stage_pair(0)}
    for mt in range(48):
        par = mt % 2
        s8g, s8u = staged.pop(mt)
        wait("tensor", sS, s8g)
        if silu_par[par] is not None:
            wait("tensor", sS, silu_par[par])
        tg = None
        for c in range(NCH):
            for n in range(2):
                tg = mark("tensor", lambda E, c_=c, n_=n, m_=mt, p_=par:
                          E.matmul(
                    out=gtgt[p_][n_],
                    lhsT=gb[m_ % 2][:, c_ * 128:(c_ + 1) * 128],
                    rhs=Rb[c_][:, n_ * 512:(n_ + 1) * 512],
                    start=(c_ == 0), stop=(c_ == NCH - 1)), sT)
        g_cons[mt] = tg
        wait("tensor", sS, s8u)
        if ucp_par[par] is not None:
            wait("tensor", sS, ucp_par[par])
        tu = None
        for c in range(NCH):
            for n in range(2):
                tu = mark("tensor", lambda E, c_=c, n_=n, m_=mt, p_=par:
                          E.matmul(
                    out=utgt[p_][:, n_ * 512:(n_ + 1) * 512],
                    lhsT=ub[m_ % 2][:, c_ * 128:(c_ + 1) * 128],
                    rhs=Rb[c_][:, n_ * 512:(n_ + 1) * 512],
                    start=(c_ == 0), stop=(c_ == NCH - 1)), sT)
        u_cons[mt] = tu
        if mt + 1 < 48:
            staged[mt + 1] = stage_pair(mt + 1)
        wait("scalar", sT, tg)
        if prev_tf is not None:
            wait("scalar", sV, prev_tf)
        if par == 0:
            prev_silu = mark("scalar", lambda E, m_=mt: E.activation(
                out=scr1, in_=psA[:], func=AF.Silu, scale=gsc_col(m_)), sS)
        else:
            for half in range(2):
                prev_silu = mark("scalar", lambda E, m_=mt, h_=half:
                                 E.activation(
                    out=scr1[:, h_ * 512:(h_ + 1) * 512], in_=gtgt[1][h_],
                    func=AF.Silu, scale=gsc_col(m_)), sS)
        silu_par[par] = prev_silu
        wait("scalar", sT, tu)
        prev_ucp = mark("scalar", lambda E, m_=mt, p_=par: E.activation(
            out=scr2, in_=utgt[p_], func=AF.Copy, scale=usc_col(m_)), sS)
        ucp_par[par] = prev_ucp
        wait("vector", sS, prev_ucp)
        if mt >= 2:
            wait("vector", sFz[mt % 2], fz_cons[mt - 2])
        tf = mark("vector", lambda E, m_=mt: E.tensor_mul(
            out=fzb[m_ % 2], in0=scr1, in1=scr2), sV)
        prev_tf = tf
        wait("sync", sV, tf)
        fz_cons[mt] = dma("sync", fuse0[mt * 128:(mt + 1) * 128, :],
                          fzb[mt % 2], sFz[mt % 2])
    dwf = f32("dw_f", 2 * 384)
    dwb = [bf("dw0", 2 * 384), bf("dw1", 2 * 384)]
    frb = [bf("fr0", TOK), bf("fr1", TOK)]
    w8d = AFP8[:, 0:768]
    ev_b = None
    fr_cons = {}
    dw_cons = {}
    dstage_cons = None
    prev_s8dn = None
    wait("sync", sFz[0], fz_cons[46])
    wait("sync", sFz[1], fz_cons[47])
    pairs = [(g, k) for g in range(6) for k in range(24)]

    def stage_dn(idx):
        nonlocal dstage_cons, prev_s8dn
        grp, kb = pairs[idx]
        slot = kb % 2
        if dstage_cons is not None:
            wait("gpsimd", sV, dstage_cons)
        t_d = dma("gpsimd", dwf, downw2[grp, kb], sDst)
        wait("vector", sDst, t_d)
        if prev_s8dn is not None:
            wait("vector", sS, prev_s8dn)
        dstage_cons = mark("vector", lambda E: E.tensor_copy(
            out=w8d, in_=dwf), sV)
        if idx >= 2 and pairs[idx - 2] in dw_cons:
            wait("scalar", sT, dw_cons[pairs[idx - 2]])
        wait("scalar", sV, dstage_cons)
        s8d = mark("scalar", lambda E, sl=slot: E.copy(
            out=dwb[sl], in_=w8d), sS)
        prev_s8dn = s8d
        return s8d

    staged_dn = stage_dn(0)
    for idx, (grp, kb) in enumerate(pairs):
        if kb == 0:
            mts = [m for m in (3 * grp, 3 * grp + 1, 3 * grp + 2) if m < NCH]
            pgrp = [psS[:, 0:1024], psS[:, 1024:2048], psA[:]]
        slot = kb % 2
        s8d = staged_dn
        for j in range(2):
            kc = 2 * kb + j
            if (grp, kb) != (0, 0) or j > 0:
                prevk2 = (grp, kc - 2)
                if kc >= 2 and prevk2 in fr_cons:
                    wait("sync", sT, fr_cons[prevk2])
                elif kc < 2 and grp > 0 and (grp - 1, 46 + kc) in fr_cons:
                    wait("sync", sT, fr_cons[(grp - 1, 46 + kc)])
            t_f = dma("sync", frb[kc % 2],
                      fuse0[kc * 128:(kc + 1) * 128, :], sFr[kc % 2])
            wait("tensor", sS, s8d)
            wait("tensor", sFr[kc % 2], t_f)
            if kc == 0 and ev_b is not None:
                wait("tensor", sV, ev_b)
            tt = None
            for mi, m in enumerate(mts):
                for n in range(2):
                    tt = mark("tensor", lambda E, mi_=mi, j_=j, n_=n,
                              k_=kc, sl=slot: E.matmul(
                        out=pgrp[mi_][:, n_ * 512:(n_ + 1) * 512],
                        lhsT=dwb[sl][:, j_ * 384 + mi_ * 128:
                                     j_ * 384 + (mi_ + 1) * 128],
                        rhs=frb[k_ % 2][:, n_ * 512:(n_ + 1) * 512],
                        start=(k_ == 0), stop=(k_ == 47)), sT)
            fr_cons[(grp, kc)] = tt
            dw_cons[(grp, kb)] = tt
        if idx + 1 < len(pairs):
            staged_dn = stage_dn(idx + 1)
        if kb == 23:
            wait("vector", sT, dw_cons[(grp, 23)])
            for mi, m in enumerate(mts):
                ev_b = mark("vector", lambda E, mi_=mi, m_=m:
                            E.scalar_tensor_tensor(
                    out=R[m_], in0=pgrp[mi_], scalar=dsc_col(m_), in1=R[m_],
                    op0=ALU.mult, op1=ALU.add), sV)
    segf.release("wstage")
    segf.release("dw_f")
    segf.release("b_scr1")
    segf.release("b_scr2")
    for nm in ("gb0", "gb1", "ub0", "ub1", "fz0", "fz1", "dw0", "dw1",
               "fr0", "fr1"):
        segb.release(nm)
    wait("scalar", sV, ev_b)
    lnv, lns = post_ln(c0r(1), R, R)

    if dbg_phase == "B":
        return finish()

    sQw = [dsem("sQw0"), dsem("sQw1")]
    sVw = [dsem("sVw0"), dsem("sVw1")]
    sKsp = [dsem("sKsp0"), dsem("sKsp1")]
    sVsp = dsem("sVsp")
    sKst = [dsem("sKst0"), dsem("sKst1")]
    sVst = [dsem("sVst0"), dsem("sVst1")]

    xn = Rb
    qbuf = bf("qbuf", NCH * TOK)
    wait("scalar", sV, lnv)
    lnv2, xn_tok = post_ln(c0r(1), R, R, dstb_chunks=xn,
                           gamma_col=lambda c: gammaap[:, c:c + 1],
                           beta_col=lambda c: betaap[:, c:c + 1])

    qkvb = [bf("qkvb0", S), bf("qkvb1", S)]
    evb = [bf("evb0", TOK), bf("evb1", TOK)]
    qkv_cons = {}
    ev_cons = {}
    prev_evs = None
    wait("gpsimd", sS, xn_tok)
    wait("sync", sS, xn_tok)
    wait("sync", sPc, t_pc)
    for mt in range(NCH, 2 * NCH):
        if mt >= NCH + 2:
            wait("sync", sT, qkv_cons[mt - 2])
        t_w = dma("sync", qkvb[mt % 2], qkvw_b[mt], sQw[mt % 2])
        wait("tensor", sQw[mt % 2], t_w)
        wait("tensor", sS, xn_tok)
        if prev_evs is not None:
            wait("tensor", sS, prev_evs)
        tmm = None
        for c in range(NCH):
            for n in range(2):
                tmm = mark("tensor", lambda E, c_=c, n_=n, m_=mt: E.matmul(
                    out=psA[:, n_ * 512:(n_ + 1) * 512],
                    lhsT=qkvb[m_ % 2][:, c_ * 128:(c_ + 1) * 128],
                    rhs=xn[c_][:, n_ * 512:(n_ + 1) * 512],
                    start=(c_ == 0), stop=(c_ == NCH - 1)), sT)
        qkv_cons[mt] = tmm
        wait("scalar", sT, tmm)
        if mt >= NCH + 2:
            wait("scalar", sKsp[mt % 2], ev_cons[mt - 2])
        prev_evs = mark("scalar", lambda E, m_=mt: E.mul(
            out=evb[m_ % 2], in_=psA[:], mul=c0(7)), sS)
        wait("scalar", sS, prev_evs)
        ev_cons[mt] = dma("scalar", ccKin[(mt - NCH) * 128:(mt - NCH + 1) * 128, :],
                          evb[mt % 2], sKsp[mt % 2])
    wait("gpsimd", sKsp[0], ev_cons[30])
    wait("gpsimd", sKsp[1], ev_cons[31])
    t_ccK = mark("gpsimd", lambda E: E.collective_compute(
        "AllGather", ALU.bypass, replica_groups=RG,
        ins=[ccKin], outs=[ccKout]), sG)
    vstage = [bf("vst0", TOK), bf("vst1", TOK)]
    vw_cons = {}
    t_vsp = 0
    prev_vev = prev_evs
    for dh in range(2):
        for tg in range(3):
            tts = [t for t in (3 * tg, 3 * tg + 1, 3 * tg + 2) if t < 8]
            pgrp = [psS[:, 0:1024], psS[:, 1024:2048], psA[:]]
            tvm = None
            for kc in range(NCH):
                sl = kc % 2
                key = (dh, tg, kc - 2)
                if key in vw_cons:
                    wait("sync", sT, vw_cons[key])
                elif (dh, tg) != (0, 0) and kc < 2:
                    pk = (dh, tg - 1, 14 + kc) if tg > 0 else (dh - 1, 2, 14 + kc)
                    if pk in vw_cons:
                        wait("sync", sT, vw_cons[pk])
                t_v = dma("sync", vstage[sl],
                          vw_b[kc, :, dh * 1024:(dh + 1) * 1024], sVw[sl])
                wait("tensor", sVw[sl], t_v)
                if kc == 0 and prev_vev is not None:
                    wait("tensor", sS, prev_vev)
                for ti, tt_ in enumerate(tts):
                    for n in range(2):
                        tvm = mark("tensor", lambda E, ti_=ti, t_=tt_, n_=n,
                                   c_=kc, sl_=sl: E.matmul(
                            out=pgrp[ti_][:, n_ * 512:(n_ + 1) * 512],
                            lhsT=xn[c_][:, t_ * 128:(t_ + 1) * 128],
                            rhs=vstage[sl_][:, n_ * 512:(n_ + 1) * 512],
                            start=(c_ == 0), stop=(c_ == NCH - 1)), sT)
                vw_cons[(dh, tg, kc)] = tvm
            for ti, tt_ in enumerate(tts):
                wait("scalar", sT, tvm)
                if (dh, tg, ti) == (0, 0, 0):
                    wait("scalar", sKsp[0], ev_cons[30])
                    wait("scalar", sKsp[1], ev_cons[31])
                else:
                    wait("scalar", sVsp, t_vsp)
                prev_vev = mark("scalar", lambda E, ti_=ti: E.copy(
                    out=evb[0], in_=pgrp[ti_]), sS)
                wait("scalar", sS, prev_vev)
                t_vsp = dma("scalar",
                            ccVin[tt_ * 128:(tt_ + 1) * 128,
                                  dh * 1024:(dh + 1) * 1024],
                            evb[0], sVsp)
    prev_evs = prev_vev
    for mt in range(NCH):
        pred = 30 if mt == 0 else (31 if mt == 1 else mt - 2)
        wait("sync", sT, qkv_cons[pred])
        t_w = dma("sync", qkvb[mt % 2], qkvw_b[mt], sQw[mt % 2])
        wait("tensor", sQw[mt % 2], t_w)
        if prev_evs is not None:
            wait("tensor", sS, prev_evs)
        tmm = None
        for c in range(NCH):
            for n in range(2):
                tmm = mark("tensor", lambda E, c_=c, n_=n, m_=mt: E.matmul(
                    out=psA[:, n_ * 512:(n_ + 1) * 512],
                    lhsT=qkvb[m_ % 2][:, c_ * 128:(c_ + 1) * 128],
                    rhs=xn[c_][:, n_ * 512:(n_ + 1) * 512],
                    start=(c_ == 0), stop=(c_ == NCH - 1)), sT)
        qkv_cons[mt] = tmm
        wait("scalar", sT, tmm)
        prev_evs = mark("scalar", lambda E, m_=mt: E.copy(
            out=qbuf[:, m_ * 1024:(m_ + 1) * 1024], in_=psA[:]), sS)
    q_last = prev_evs
    wait("gpsimd", sVsp, t_vsp)
    t_ccV = mark("gpsimd", lambda E: E.collective_compute(
        "AllGather", ALU.bypass, replica_groups=RG,
        ins=[ccVin], outs=[ccVout]), sG)

    segb.release("vst0")
    segb.release("vst1")
    segb.release("evb0")
    segb.release("evb1")
    Kst = [bf("Kst0", S), bf("Kst1", S)]
    segb.release("qkvb0")
    segb.release("qkvb1")
    for c in range(NCH):
        segb.release(f"Rb{c}")
    O1 = bf("O1", NCH * TOK)
    Vst = [bf("Vst0", S), bf("Vst1", S)]
    PT1 = bf("PT1", NCH * 128)
    Otk = bf("Otk", S)
    Sf1 = f32("Sf1", S)
    P1 = bf("P1", S)
    diag1 = bf("diag1", 128)
    cols1 = f32("cols1", 16)
    negmax1 = cols1[:, 0:1]
    rowsum1 = cols1[:, 1:2]
    rinv1 = cols1[:, 2:3]
    kst_cons = {}
    vst_cons = {}
    m1_cons = {}
    prev_sh1 = None
    prev_pt1 = None
    prev_opv = None
    o1_done = None
    wait("sync", sG, t_ccK)
    wait("tensor", sS, q_last)
    wait("gpsimd", sG, t_ccV)
    wait("gpsimd", sS, q_last)

    def emit_pt1(qt_, diag_tok, ntile):
        nonlocal psX_free, prev_pt1
        trt = None
        for g0 in range(0, ntile, 8):
            gn = min(8, ntile - g0)
            wait("tensor", sV, diag_tok)
            if psX_free is not None:
                wait("tensor", sS, psX_free)
            for j in range(gn):
                kc = g0 + j
                trt = mark("tensor", lambda E_, k_=kc, j_=j: E_.matmul(
                    out=psA[:, j_ * 128:(j_ + 1) * 128],
                    lhsT=P1[:, k_ * 128:(k_ + 1) * 128], rhs=diag1,
                    start=True, stop=True), sT)
            wait("scalar", sT, trt)
            psX_free = mark("scalar", lambda E_, g0_=g0, g_=gn: E_.copy(
                out=PT1[:, g0_ * 128:(g0_ + g_) * 128],
                in_=psA[:, 0:g_ * 128]), sS)
        prev_pt1 = trt

    def emit_pv1(qt_, ntile):
        nonlocal prev_opv, psX_free, o1_done, prev_pt1
        wait("tensor", sS, psX_free)
        tpv = None
        for kc in range(ntile):
            sl = kc % 2
            if kc >= 2:
                pred = (qt_, kc - 2)
            elif qt_ >= 1:
                np_ = 2 * qt_
                pred = (qt_ - 1, np_ - 1 if (np_ - 1) % 2 == sl else np_ - 2)
            else:
                pred = None
            if pred is not None and pred in vst_cons:
                wait("gpsimd", sT, vst_cons[pred])
            if kc <= qt_:
                row = kc * 128
            else:
                row = TOK + (kc - qt_ - 1) * 128
            t_v1 = dma("gpsimd", Vst[sl], ccVout[row:row + 128, :], sVst[sl])
            wait("tensor", sVst[sl], t_v1)
            for n in range(4):
                tpv = mark("tensor", lambda E_, c_=kc, n_=n, sl_=sl: E_.matmul(
                    out=psS[:, n_ * 512:(n_ + 1) * 512],
                    lhsT=PT1[:, c_ * 128:(c_ + 1) * 128],
                    rhs=Vst[sl_][:, n_ * 512:(n_ + 1) * 512],
                    start=(c_ == 0), stop=(c_ == ntile - 1)), sT)
            vst_cons[(qt_, kc)] = tpv
        wait("scalar", sT, tpv)
        prev_opv = mark("scalar", lambda E_: E_.copy(out=Otk, in_=psS[:]), sS)
        trt = None
        for g0 in range(0, NCH, 8):
            wait("tensor", sS, prev_opv)
            wait("tensor", sS, psX_free)
            for j in range(8):
                mt = g0 + j
                trt = mark("tensor", lambda E_, m_=mt, j_=j: E_.matmul(
                    out=psA[:, j_ * 128:(j_ + 1) * 128],
                    lhsT=Otk[:, m_ * 128:(m_ + 1) * 128], rhs=identb_ap,
                    start=True, stop=True), sT)
            wait("scalar", sT, trt)
            dst = O1[:, qt_ * 2048 + g0 * 128:qt_ * 2048 + (g0 + 8) * 128]
            psX_free = mark("scalar", lambda E_, d=dst: E_.copy(
                out=d, in_=psA[:]), sS)
        o1_done = psX_free
        prev_pt1 = trt

    for qt in range(8):
        eoff = 128 * (qt + 1)
        keys = 2 * eoff
        ntile = keys // 128
        sls = []
        soff = 0
        while soff < keys:
            w = min(512, keys - soff)
            sls.append((soff, w))
            soff += w
        iE = (eoff - 128) // 512
        iO = (keys - 128) // 512
        adds = {i: [] for i in range(len(sls))}
        adds[iE].append(("tri", (eoff - 128) - sls[iE][0]))
        adds[iO].append(("oad", None))
        tqk = None
        for kc in range(NCH):
            sl = kc % 2
            key = qt * 16 + kc - 2
            if key >= 0 and key in kst_cons:
                wait("sync", sT, kst_cons[key])
            t_k1 = dma("sync", Kst[sl][:, 0:eoff],
                       ccKout[kc * 128:(kc + 1) * 128, 0:eoff], sKst[sl])
            t_k2 = dma("sync", Kst[sl][:, eoff:keys],
                       ccKout[E + kc * 128:E + (kc + 1) * 128, 0:eoff], sKst[sl])
            wait("tensor", sKst[sl], t_k2)
            if kc == 0 and prev_opv is not None:
                wait("tensor", sS, prev_opv)
            for i, (off, w) in enumerate(sls):
                tqk = mark("tensor", lambda E_, c_=kc, o_=off, w_=w,
                           q_=qt, sl_=sl, op_=(not adds[i]): E_.matmul(
                    out=psS[:, o_:o_ + w_],
                    lhsT=qbuf[:, c_ * 1024 + q_ * 128:c_ * 1024 + (q_ + 1) * 128],
                    rhs=Kst[sl_][:, o_:o_ + w_],
                    start=(c_ == 0),
                    stop=(c_ == NCH - 1 and op_)), sT)
            kst_cons[qt * 16 + kc] = tqk
        for i, (off, w) in enumerate(sls):
            for j, (kind, p_) in enumerate(adds[i]):
                rhs = (CSTB[:, 1408 - p_:1408 - p_ + w] if kind == "tri"
                       else CSTB[:, 2048 - w:2048])
                tqk = mark("tensor", lambda E_, o_=off, w_=w, m=rhs,
                           ls=(j == len(adds[i]) - 1): E_.matmul(
                    out=psS[:, o_:o_ + w_], lhsT=identb_ap,
                    rhs=m, start=False, stop=ls), sT)
        m1_cons[qt] = tqk
        wait("vector", sT, tqk)
        if prev_sh1 is not None:
            wait("vector", sS, prev_sh1[-1])
        tcmb = mark("vector", lambda E_, k_=keys: E_.reduce_max(
            out=negmax1, in_=psS[:, 0:k_], axis=AX.X, negate=True), sV)
        wait("scalar", sV, tcmb)
        if prev_pt1 is not None:
            wait("scalar", sT, prev_pt1)
        prev_sh1 = []
        soff = 0
        while soff < keys:
            w = min(512, keys - soff)
            tshn = mark("scalar", lambda E_, o_=soff, w_=w: E_.activation(
                out=P1[:, o_:o_ + w_], in_=psS[:, o_:o_ + w_],
                func=AF.Identity, bias=negmax1), sS)
            prev_sh1.append(tshn)
            soff += w
        tex = mark("scalar", lambda E_, k_=keys: E_.activation(
            out=P1[:, 0:k_], in_=P1[:, 0:k_], func=AF.Exp,
            accum_out=rowsum1), sS)
        wait("vector", sS, tex)
        trc = mark("vector", lambda E_: E_.reciprocal(out=rinv1, in_=rowsum1), sV)
        if prev_pt1 is not None:
            wait("vector", sT, prev_pt1)
        tdg = mark("vector", lambda E_: E_.tensor_scalar(
            out=diag1, in0=identb_ap, scalar1=rinv1, scalar2=None,
            op0=ALU.mult), sV)
        wait("tensor", sS, prev_sh1[-1])
        emit_pt1(qt, tdg, ntile)
        emit_pv1(qt, ntile)
    for nm in ("Kst0", "Kst1", "Vst0", "Vst1", "PT1", "Otk", "P1",
               "qbuf", "diag1"):
        segb.release(nm)
    segf.release("Sf1")
    ow = [bf("ow0", NCH * 128), bf("ow1", NCH * 128)]
    ow_cons = {}
    ev1_cons = {}
    ev_tok1 = None
    wait("tensor", sS, o1_done)
    wait("gpsimd", sT, prev_pt1)
    wait("gpsimd", sS, o1_done)
    for mt in range(NCH):
        if mt >= 2:
            wait("gpsimd", sT, ow_cons[mt - 2])
        t_w = dma("gpsimd", ow[mt % 2], oprojw[mt], sWo[mt % 2])
        wait("tensor", sWo[mt % 2], t_w)
        if mt == 0:
            wait("tensor", sS, prev_opv)
        else:
            wait("tensor", sV, ev_tok1)
        tmm = None
        o1v = O1.rearrange("p (q x) -> p q x", x=2048)
        for kc in range(NCH):
            for n in range(2):
                tmm = mark("tensor", lambda E, k_=kc, n_=n, m_=mt, v=o1v: E.matmul(
                    out=psA[:, n_ * 512:(n_ + 1) * 512],
                    lhsT=ow[m_ % 2][:, k_ * 128:(k_ + 1) * 128],
                    rhs=v[:, n_ * 4:(n_ + 1) * 4, k_ * 128:(k_ + 1) * 128],
                    start=(k_ == 0), stop=(k_ == NCH - 1)), sT)
        ow_cons[mt] = tmm
        wait("vector", sT, tmm)
        ev_tok1 = mark("vector", lambda E, m_=mt: E.scalar_tensor_tensor(
            out=R[m_], in0=R[m_], scalar=2.0, in1=psA[:],
            op0=ALU.mult, op1=ALU.add), sV)
        ev1_cons[mt] = ev_tok1
    for nm in ("O1", "ow0", "ow1"):
        segb.release(nm)
    segf.release("cols1")
    wait("scalar", sV, ev_tok1)
    lnv, lns = post_ln(c0r(1), R, R)

    if dbg_phase == "C":
        return finish()

    meanB = f32("d_meanB", TOK)
    invB = f32("d_invB", TOK)
    d_scr = f32("d_scr", TOK)
    maxacc = f32("d_max", TOK)
    sB = f32("d_sB", TOK)
    invsB = sB
    tbq = [bf("d_tb", TOK), bf("d_tb2", TOK)]
    sqq = [bf("d_sq", TOK), bf("d_sq2", TOK)]
    wait("scalar", sV, lnv)
    mm_tok = None
    for c in range(NCH):
        tc1 = mark("scalar", lambda E, s=R[c], d=tbq[c % 2]: E.copy(out=d, in_=s), sS)
        tc2 = mark("scalar", lambda E, s=R[c], d=sqq[c % 2]: E.activation(
            out=d, in_=s, func=AF.Square), sS)
        wait("tensor", sS, tc2)
        for n in range(2):
            mm_tok = mark("tensor", lambda E, c_=c, n_=n, t=tbq[c % 2]: E.matmul(
                out=psS[0:1, n_ * 512:(n_ + 1) * 512], lhsT=onesb_ap,
                rhs=t[:, n_ * 512:(n_ + 1) * 512],
                start=(c_ == 0), stop=(c_ == NCH - 1)), sT)
            mm_tok = mark("tensor", lambda E, c_=c, n_=n, q=sqq[c % 2]: E.matmul(
                out=psS[0:1, 1024 + n_ * 512:1024 + (n_ + 1) * 512], lhsT=onesb_ap,
                rhs=q[:, n_ * 512:(n_ + 1) * 512],
                start=(c_ == 0), stop=(c_ == NCH - 1)), sT)
        if c >= 1:
            wait("scalar", sT, mm_tok)
    mean_r = meanB[0:1, 0:TOK]
    var_r = invB[0:1, 0:TOK]
    wait("scalar", sT, mm_tok)
    t1 = mark("scalar", lambda E: E.mul(out=mean_r, in_=psS[0:1, 0:1024],
                                        mul=c0r(0)), sS)
    t1 = mark("scalar", lambda E: E.activation(out=var_r, in_=mean_r,
                                               func=AF.Square), sS)
    wait("vector", sS, t1)
    t2 = mark("vector", lambda E: E.scalar_tensor_tensor(
        out=var_r, in0=psS[0:1, 1024:2048], scalar=c0r(0), in1=var_r,
        op0=ALU.mult, op1=ALU.subtract), sV)
    wait("scalar", sV, t2)
    t3 = mark("scalar", lambda E: E.activation(out=var_r, in_=var_r,
                                               func=AF.Ln, bias=c0r(2)), sS)
    t3 = mark("scalar", lambda E: E.activation(out=var_r, in_=var_r,
                                               func=AF.Exp, scale=c0r(3)), sS)
    wait("tensor", sS, t3)
    tb_tok = None
    for n in range(2):
        tb_tok = mark("tensor", lambda E, n_=n: E.matmul(
            out=psS[:, n_ * 512:(n_ + 1) * 512], lhsT=onesf_ap,
            rhs=mean_r[0:1, n_ * 512:(n_ + 1) * 512], start=True, stop=True), sT)
        tb_tok = mark("tensor", lambda E, n_=n: E.matmul(
            out=psS[:, 1024 + n_ * 512:1024 + (n_ + 1) * 512], lhsT=onesf_ap,
            rhs=var_r[0:1, n_ * 512:(n_ + 1) * 512], start=True, stop=True), sT)
    wait("scalar", sT, tb_tok)
    t4 = mark("scalar", lambda E: E.copy(out=meanB, in_=psS[:, 0:1024]), sS)
    t4 = mark("scalar", lambda E: E.copy(out=invB, in_=psS[:, 1024:2048]), sS)
    minacc = sB
    tmx = None
    for c in range(NCH):
        wait("vector", sS, t4)
        mark("vector", lambda E, s=R[c]: E.tensor_sub(out=d_scr, in0=s, in1=meanB), sV)
        mark("vector", lambda E: E.tensor_mul(out=d_scr, in0=d_scr, in1=invB), sV)
        if c == 0:
            tmx = mark("vector", lambda E: E.tensor_copy(out=maxacc, in_=d_scr), sV)
            tmx = mark("vector", lambda E: E.tensor_copy(out=minacc, in_=d_scr), sV)
        else:
            tmx = mark("vector", lambda E: E.tensor_max(
                out=maxacc, in0=maxacc, in1=d_scr), sV)
            tmx = mark("vector", lambda E: E.tensor_tensor(
                out=minacc, in0=minacc, in1=d_scr, op=ALU.min), sV)
    tmx = mark("vector", lambda E: E.tensor_scalar(
        out=minacc, in0=minacc, scalar1=-1.0, scalar2=None, op0=ALU.mult), sV)
    tmx = mark("vector", lambda E: E.tensor_max(
        out=maxacc, in0=maxacc, in1=minacc), sV)
    mxc = maxacc[:, 0:8]
    srow8 = maxacc[:, 128:256]
    srowf = d_scr
    trt = None
    for g in range(2):
        wait("tensor", sV, tmx)
        if g > 0:
            wait("tensor", sV, trr)
        for j in range(4):
            t_i = g * 4 + j
            trt = mark("tensor", lambda E, t=t_i, j_=j: E.transpose(
                out=psA[:, j_ * 128:(j_ + 1) * 128],
                in_=maxacc[:, t * 128:(t + 1) * 128], identity=identf_ap), sT)
        wait("vector", sT, trt)
        trr = None
        for j in range(4):
            t_i = g * 4 + j
            trr = mark("vector", lambda E, t=t_i, j_=j: E.reduce_max(
                out=mxc[:, t:t + 1], in_=psA[:, j_ * 128:(j_ + 1) * 128],
                axis=AX.X), sV)
    wait("tensor", sV, trr)
    trt = mark("tensor", lambda E: E.transpose(
        out=psA[0:8, 0:128], in_=mxc[:, 0:8], identity=identf_ap), sT)
    wait("scalar", sT, trt)
    t5 = mark("scalar", lambda E: E.copy(out=srow8[0:8, :], in_=psA[0:8, 0:128]), sS)
    wait("sync", sS, t5)
    t5d = dma("sync", srowf[0:1, 0:1024], srow8[0:8, 0:128], sGrel)
    wait("tensor", sGrel, t5d)
    tb_tok = None
    for n in range(2):
        tb_tok = mark("tensor", lambda E, n_=n: E.matmul(
            out=psS[:, n_ * 512:(n_ + 1) * 512], lhsT=onesf_ap,
            rhs=srowf[0:1, n_ * 512:(n_ + 1) * 512], start=True, stop=True), sT)
    wait("vector", sT, tb_tok)
    t6 = mark("vector", lambda E: E.reciprocal(out=sB, in_=psS[:, 0:1024]), sV)
    t6 = mark("vector", lambda E: E.tensor_scalar(
        out=sB, in0=sB, scalar1=127.0, scalar2=None, op0=ALU.mult), sV)
    if dbg_phase == "D0":
        wait("sync", sS, sS.n)
        wait("sync", sV, sV.n)
        tdone = dma("sync", out[0:128, :], sB, sO)
        tdone = dma("sync", out[128:256, :], maxacc, sO)
        tdone = dma("sync", out[256:384, :], meanB, sO)
        tdone = dma("sync", out[384:512, :], invB, sO)
        tdone = dma("sync", out[512:640, :], d_scr, sO)
        wait("sync", sO, tdone)
        p_finalize(nc, p, ctx)
        return nc
    xq = [bf(f"xq{c}", TOK) for c in range(NCH)]
    i8t = AI8[:, 0:1024]
    xq_tok = None
    for c in range(NCH):
        mark("vector", lambda E, s=R[c]: E.tensor_sub(out=d_scr, in0=s, in1=meanB), sV)
        mark("vector", lambda E: E.tensor_mul(out=d_scr, in0=d_scr, in1=invB), sV)
        tvq = mark("vector", lambda E: E.tensor_mul(out=d_scr, in0=d_scr, in1=sB), sV)
        wait("vector", sS, xq_tok) if xq_tok else None
        tvq = mark("vector", lambda E: E.tensor_copy(out=i8t, in_=d_scr), sV)
        wait("scalar", sV, tvq)
        xq_tok = mark("scalar", lambda E, c_=c: E.copy(out=xq[c_], in_=i8t), sS)
    if dbg_phase == "D1":
        tdone = None
        wait("gpsimd", sS, sS.n)
        wait("gpsimd", sV, sV.n)
        for c in range(NCH):
            tdone = dma("gpsimd", out[c * 128:(c + 1) * 128, :], xq[c], sO)
        wait("gpsimd", sO, tdone)
        p_finalize(nc, p, ctx)
        return nc
    t6 = mark("vector", lambda E: E.reciprocal(out=sB, in_=sB), sV)
    gwb = [bf("gwb0", S), bf("gwb1", S)]
    uwb = [bf("uwb0", S), bf("uwb1", S)]
    fz1b = [bf("fz1b0", TOK), bf("fz1b1", TOK)]
    gw_cons = {}
    uw_cons = {}
    fz1_cons = {}
    prev_g1 = None
    prev_u1 = None
    sDg = [dsem("sDg0"), dsem("sDg1")]
    sDu = [dsem("sDu0"), dsem("sDu1")]
    wait("gpsimd", sS, xq_tok)
    for ic in range(64):
        sl = ic % 2
        if ic >= 2:
            wait("gpsimd", sT, gw_cons[ic - 2])
        t_g = dma("gpsimd", gwb[sl], wg1[ic], sDg[sl])
        wait("scalar", sDg[sl], t_g)
        sg = mark("scalar", lambda E, sl_=sl: E.sign(out=gwb[sl_], in_=gwb[sl_]), sS)
        if ic >= 2:
            wait("gpsimd", sT, uw_cons[ic - 2])
        t_u = dma("gpsimd", uwb[sl], wu1[ic], sDu[sl])
        wait("scalar", sDu[sl], t_u)
        su = mark("scalar", lambda E, sl_=sl: E.sign(out=uwb[sl_], in_=uwb[sl_]), sS)
        wait("tensor", sS, sg)
        wait("tensor", sS, xq_tok)
        if prev_g1 is not None:
            wait("tensor", sV, prev_g1)
        tg = None
        for c in range(NCH):
            for n in range(2):
                tg = mark("tensor", lambda E, c_=c, n_=n, sl_=sl: E.matmul(
                    out=psA[:, n_ * 512:(n_ + 1) * 512],
                    lhsT=gwb[sl_][:, c_ * 128:(c_ + 1) * 128],
                    rhs=xq[c_][:, n_ * 512:(n_ + 1) * 512],
                    start=(c_ == 0), stop=(c_ == NCH - 1)), sT)
        gw_cons[ic] = tg
        wait("tensor", sS, su)
        if prev_u1 is not None:
            wait("tensor", sV, prev_u1)
        tu = None
        for c in range(NCH):
            for n in range(2):
                tu = mark("tensor", lambda E, c_=c, n_=n, sl_=sl: E.matmul(
                    out=psS[:, n_ * 512:(n_ + 1) * 512],
                    lhsT=uwb[sl_][:, c_ * 128:(c_ + 1) * 128],
                    rhs=xq[c_][:, n_ * 512:(n_ + 1) * 512],
                    start=(c_ == 0), stop=(c_ == NCH - 1)), sT)
        uw_cons[ic] = tu
        wait("vector", sT, tg)
        prev_g1 = mark("vector", lambda E: E.scalar_tensor_tensor(
            out=d_scr, in0=psA[:], scalar=1.0, in1=invsB,
            op0=ALU.mult, op1=ALU.mult), sV)
        wait("scalar", sV, prev_g1)
        sgs = mark("scalar", lambda E: E.activation(out=d_scr, in_=d_scr,
                                                    func=AF.Silu), sS)
        wait("vector", sT, tu)
        wait("vector", sS, sgs)
        prev_u1 = mark("vector", lambda E: E.scalar_tensor_tensor(
            out=maxacc, in0=psS[:, 0:1024], scalar=1.0, in1=invsB,
            op0=ALU.mult, op1=ALU.mult), sV)
        if ic >= 2:
            wait("vector", sFz[sl], fz1_cons[ic - 2])
        tf = mark("vector", lambda E, sl_=sl: E.tensor_mul(
            out=fz1b[sl_], in0=d_scr, in1=maxacc), sV)
        wait("sync", sV, tf)
        fz1_cons[ic] = dma("sync", fuse1[ic * 128:(ic + 1) * 128, :],
                           fz1b[sl], sFz[sl])
    dwb1 = [bf("dw1b0", 2 * 384), bf("dw1b1", 2 * 384)]
    fr1b = [bf("fr1b0", TOK), bf("fr1b1", TOK)]
    ev_d = None
    fr_cons1 = {}
    dw_cons1 = {}
    wait("sync", sFz[0], fz1_cons[62])
    wait("sync", sFz[1], fz1_cons[63])
    pairs1 = [(g, k) for g in range(6) for k in range(32)]

    def stage_dn1(idx):
        grp, kb = pairs1[idx]
        sl = kb % 2
        if idx >= 2 and pairs1[idx - 2] in dw_cons1:
            wait("gpsimd", sT, dw_cons1[pairs1[idx - 2]])
        t_d = dma("gpsimd", dwb1[sl], wd2[grp, kb], sDw[sl])
        wait("scalar", sDw[sl], t_d)
        return mark("scalar", lambda E, sl_=sl: E.sign(
            out=dwb1[sl_], in_=dwb1[sl_]), sS)

    staged_dn1 = stage_dn1(0)
    for idx, (grp, kb) in enumerate(pairs1):
        if kb == 0:
            mts = [m for m in (3 * grp, 3 * grp + 1, 3 * grp + 2) if m < NCH]
            pgrp = [psS[:, 0:1024], psS[:, 1024:2048], psA[:]]
        sl = kb % 2
        s8d = staged_dn1
        tt = None
        for j in range(2):
            kc = 2 * kb + j
            if kc >= 2 and (grp, kc - 2) in fr_cons1:
                wait("sync", sT, fr_cons1[(grp, kc - 2)])
            elif kc < 2 and grp > 0 and (grp - 1, 62 + kc) in fr_cons1:
                wait("sync", sT, fr_cons1[(grp - 1, 62 + kc)])
            t_f = dma("sync", fr1b[kc % 2],
                      fuse1[kc * 128:(kc + 1) * 128, :], sFr[kc % 2])
            wait("tensor", sS, s8d)
            wait("tensor", sFr[kc % 2], t_f)
            if kc == 0 and ev_d is not None:
                wait("tensor", sV, ev_d)
            for mi, m in enumerate(mts):
                for n in range(2):
                    tt = mark("tensor", lambda E, mi_=mi, j_=j, n_=n,
                              k_=kc, sl_=sl: E.matmul(
                        out=pgrp[mi_][:, n_ * 512:(n_ + 1) * 512],
                        lhsT=dwb1[sl_][:, j_ * 384 + mi_ * 128:
                                       j_ * 384 + (mi_ + 1) * 128],
                        rhs=fr1b[k_ % 2][:, n_ * 512:(n_ + 1) * 512],
                        start=(k_ == 0), stop=(k_ == 63)), sT)
            fr_cons1[(grp, kc)] = tt
        dw_cons1[(grp, kb)] = tt
        if idx + 1 < len(pairs1):
            staged_dn1 = stage_dn1(idx + 1)
        if kb == 31:
            wait("vector", sT, dw_cons1[(grp, 31)])
            for mi, m in enumerate(mts):
                ev_d = mark("vector", lambda E, mi_=mi, m_=m:
                            E.scalar_tensor_tensor(
                    out=R[m_], in0=pgrp[mi_], scalar=1.0, in1=R[m_],
                    op0=ALU.mult, op1=ALU.add), sV)
    for nm in ("d_meanB", "d_invB", "d_scr", "d_max", "d_sB"):
        segf.release(nm)
    wait("scalar", sV, ev_d)
    lnv, lns = post_ln(c0r(1), R, R)
    return finish()


def p_finalize(nc, p, ctx):
    with nc.Block() as block:
        p.run(block)
    ctx.close()


def _stage_inputs(L0_attn_x, L0_attn_Wq, L0_attn_Wkv, L0_attn_Wkvh, L0_attn_Wg,
                  L0_attn_Wout, L0_ffn_gate_w, L0_ffn_up_w, L0_ffn_down_w,
                  L0_ffn_gate_scale, L0_ffn_up_scale, L0_ffn_down_scale,
                  L1_attn_qkv, L1_attn_o_proj, L1_attn_gamma, L1_attn_beta,
                  L1_ffn_W_g, L1_ffn_W_u, L1_ffn_W_d):
    f32c = lambda a: np.ascontiguousarray(a, dtype=np.float32)
    wq_st = rearr_lhsT(f32c(L0_attn_Wq.T))
    wkv_st = rearr_lhsT(f32c(L0_attn_Wkv.T))
    kvg = np.concatenate([f32c(L0_attn_Wkvh), f32c(L0_attn_Wg)], axis=0)
    wkvg_st = rearr_lhsT(kvg.T, j=64)[0]
    wout_st = rearr_lhsT(f32c(L0_attn_Wout.T))
    gatew_st = rearr_lhsT(f32c(L0_ffn_gate_w))
    upw_st = rearr_lhsT(f32c(L0_ffn_up_w))
    def group_down(wmat, nkb):
        Wp = np.concatenate(
            [f32c(wmat), np.zeros((wmat.shape[0], 2304 - wmat.shape[1]),
                                  np.float32)], axis=1)
        a = Wp.reshape(nkb, 2, 128, 6, 384).transpose(3, 0, 2, 1, 4)
        return np.ascontiguousarray(a.reshape(6, nkb, 128, 768))
    downw2_st = group_down(L0_ffn_down_w, 24)
    qkvw_st = rearr_lhsT(f32c(L1_attn_qkv[:, 0:2 * E]))
    vweight_st = np.ascontiguousarray(
        f32c(L1_attn_qkv[:, 2 * E:3 * E]).reshape(NCH, 128, E))
    oprojw_st = rearr_lhsT(f32c(L1_attn_o_proj))
    wg1_st = rearr_lhsT(f32c(L1_ffn_W_g.T))
    wu1_st = rearr_lhsT(f32c(L1_ffn_W_u.T))
    wd2_st = group_down(np.ascontiguousarray(f32c(L1_ffn_W_d).T), 32)

    colmaj = lambda v, n: np.ascontiguousarray(
        f32c(v).reshape(n, 128).T)
    gsc_st = colmaj(L0_ffn_gate_scale, 48)
    usc_st = colmaj(L0_ffn_up_scale, 48)
    dsc_st = colmaj(L0_ffn_down_scale, 16)
    gb_st = np.concatenate([colmaj(L1_attn_gamma, 16), colmaj(L1_attn_beta, 16)],
                           axis=1)

    pos = np.arange(S, dtype=np.float64)
    inv = 10000.0 ** (-np.arange(0, D, 2, dtype=np.float64) / D)
    ang = pos[None, :] * inv[:, None]
    cosf = np.cos(ang)
    sinf = np.sin(ang)
    cosQ_full = np.concatenate([cosf, cosf], axis=0).astype(np.float32)
    sinQ_full = np.concatenate([-sinf, sinf], axis=0).astype(np.float32)
    sc = 1.0 / np.sqrt(D)
    cosK_full = (np.concatenate([cosf, cosf], axis=0) * sc).astype(np.float32)
    sinK_full = (np.concatenate([-sinf, sinf], axis=0) * sc).astype(np.float32)

    identb_st = np.eye(128, dtype=ml_dtypes.bfloat16)
    identf_st = np.eye(128, dtype=np.float32)
    onesb_st = np.ones((128, 1), dtype=ml_dtypes.bfloat16)
    onesf_st = np.ones((1, 128), dtype=np.float32)
    consts_st = np.tile(np.array(
        [[1.0 / 2048, 1e-5, 1e-8, -0.5, 0.0, 1.0 / 127, 127.0,
          1.0 / np.sqrt(2048.0)]], np.float32), (128, 1))

    shared = dict(
        wq=wq_st, wkv=wkv_st, wkvg=wkvg_st, wout=wout_st, gatew=gatew_st,
        upw=upw_st, downw2=downw2_st, gsc=gsc_st, usc=usc_st, dsc=dsc_st,
        qkvw=qkvw_st, vweight=vweight_st, oprojw=oprojw_st, gammab=gb_st,
        wg1=wg1_st, wu1=wu1_st, wd2=wd2_st, identb=identb_st,
        identf=identf_st, onesb=onesb_st, onesf=onesf_st, consts=consts_st)

    in_maps = []
    x = f32c(L0_attn_x)
    for c in range(8):
        b, half = c // 2, c % 2
        own = slice(half * TOK, (half + 1) * TOK)
        xb_ = x[b].T
        x0T_st = np.ascontiguousarray(xb_)
        qpos = (np.arange(8)[:, None] * 256 + half * 128 +
                np.arange(128)[None, :]).ravel()
        x0To_st = np.ascontiguousarray(xb_[:, qpos])
        cosQ_st = np.ascontiguousarray(cosQ_full[:, qpos])
        sinQ_st = np.ascontiguousarray(sinQ_full[:, qpos])
        cosK_st = np.ascontiguousarray(cosK_full)
        sinK_st = np.ascontiguousarray(sinK_full)
        p_ = np.arange(128)
        tri = np.where(p_[None, :] <= p_[:, None], 0.0, NEG)
        z128 = np.zeros((128, 128))
        neg128 = np.full((128, 128), NEG)
        if half == 0:
            mb = np.concatenate([tri, neg128], axis=1)
            etail, otail = tri, neg128
        else:
            mb = np.concatenate([z128, tri], axis=1)
            etail, otail = z128, tri
        z256 = np.zeros((128, 256))
        z384 = np.zeros((128, 384))
        maskb_st = np.ascontiguousarray(np.concatenate(
            [mb, z256, mb,
             z384, etail,
             z384, otail], axis=1).astype(ml_dtypes.bfloat16))
        in_maps.append(dict(shared, x0T=x0T_st, x0To=x0To_st, cosQ=cosQ_st,
                            sinQ=sinQ_st, cosK=cosK_st, sinK=sinK_st,
                            maskb=maskb_st))
    return in_maps


def kernel(**inputs):
    in_maps = _stage_inputs(**inputs)
    nc = build_nc()
    res = run_bass_kernel_spmd(nc, in_maps, core_ids=list(range(8)))
    outp = np.empty((B, S, E), np.float32)
    qpos_h = [(np.arange(8)[:, None] * 256 + h_ * 128 +
               np.arange(128)[None, :]).ravel() for h_ in range(2)]
    for c in range(8):
        b, half = c // 2, c % 2
        o = np.asarray(res.results[c]["out"])
        outp[b, qpos_h[half], :] = o.T
    return outp


# version 11
# speedup vs baseline: 1.0738x; 1.0738x over previous
import numpy as np
import ml_dtypes
from contextlib import ExitStack

import concourse.bass as bass
import concourse.mybir as mybir
from concourse.bass_utils import run_bass_kernel_spmd

F32 = mybir.dt.float32
BF16 = mybir.dt.bfloat16
FP8 = mybir.dt.float8e4
I8 = mybir.dt.int8
AF = mybir.ActivationFunctionType
ALU = mybir.AluOpType
AX = mybir.AxisListType

B, S, E = 4, 2048, 2048
H, D = 16, 128
TOK = 1024
NCH = 16
NEG = -3.0e38
CLAMP = -80.0

KB = 1024


class Seg:

    def __init__(self, name, nbytes):
        self.name = name
        self.nbytes = nbytes
        self.free = [(0, nbytes)]
        self.live = {}

    def alloc(self, name, nbytes, align=4):
        for i, (off, ln) in enumerate(self.free):
            aoff = (off + align - 1) // align * align
            if aoff + nbytes <= off + ln:
                pre = (off, aoff - off)
                post = (aoff + nbytes, off + ln - (aoff + nbytes))
                repl = [p for p in (pre, post) if p[1] > 0]
                self.free[i:i + 1] = repl
                self.live[name] = (aoff, nbytes)
                return aoff
        raise MemoryError(
            f"{self.name}: cannot alloc {name} ({nbytes}B/part); "
            f"live={ {k: v for k, v in self.live.items()} }")

    def release(self, name):
        off, ln = self.live.pop(name)
        self.free.append((off, ln))
        self.free.sort()
        merged = []
        for seg in self.free:
            if merged and merged[-1][0] + merged[-1][1] == seg[0]:
                merged[-1] = (merged[-1][0], merged[-1][1] + seg[1])
            else:
                merged.append(list(seg))
        self.free = [tuple(s) for s in merged]


class Prog:
    ENGINES = ("sync", "tensor", "vector", "scalar", "gpsimd")

    def __init__(self, nc):
        self.nc = nc
        self.ops = {e: [] for e in self.ENGINES}

    def emit(self, eng, fn):
        self.ops[eng].append(fn)

    def run(self, block):
        ops = self.ops

        @block.sync
        def _(e):
            for f in ops["sync"]:
                f(e)

        @block.tensor
        def _(e):
            for f in ops["tensor"]:
                f(e)

        @block.vector
        def _(e):
            for f in ops["vector"]:
                f(e)

        @block.scalar
        def _(e):
            for f in ops["scalar"]:
                f(e)

        @block.gpsimd
        def _(e):
            for f in ops["gpsimd"]:
                f(e)


class Sem:

    def __init__(self, handle):
        self.h = handle
        self.n = 0

    def bump(self, k):
        self.n += k
        return self.n


def rearr_lhsT(wT, p=128, j=128):
    K, M = wT.shape
    kc, mt = K // p, M // j
    a = wT.reshape(kc, p, mt, j).transpose(2, 1, 0, 3)
    return np.ascontiguousarray(a.reshape(mt, p, kc * j))


def build_nc(dbg_phase=None):
    nc = bass.Bass("TRN2", target_bir_lowering=False, debug=False)
    dt = nc.dram_tensor

    x0T = dt("x0T", [E, S], F32, kind="ExternalInput").ap()
    x0To = dt("x0To", [E, TOK], F32, kind="ExternalInput").ap()
    wq = dt("wq", [H, 128, E], F32, kind="ExternalInput").ap()
    wkv = dt("wkv", [2, 128, E], F32, kind="ExternalInput").ap()
    wkvg = dt("wkvg", [128, E // 2], F32, kind="ExternalInput").ap()
    wout = dt("wout", [NCH, 128, E], F32, kind="ExternalInput").ap()
    gatew = dt("gatew", [48, 128, E], F32, kind="ExternalInput").ap()
    upw = dt("upw", [48, 128, E], F32, kind="ExternalInput").ap()
    downw2 = dt("downw2", [4, 24, 128, 1024], F32, kind="ExternalInput").ap()
    gsc = dt("gsc", [128, 48], F32, kind="ExternalInput").ap()
    usc = dt("usc", [128, 48], F32, kind="ExternalInput").ap()
    dsc = dt("dsc", [128, NCH], F32, kind="ExternalInput").ap()
    qkvw = dt("qkvw", [32, 128, E], F32, kind="ExternalInput").ap()
    oprojw = dt("oprojw", [NCH, 128, E], F32, kind="ExternalInput").ap()
    gammab = dt("gammab", [128, 2 * NCH], F32, kind="ExternalInput").ap()
    wg1 = dt("wg1", [64, 128, E], F32, kind="ExternalInput").ap()
    wu1 = dt("wu1", [64, 128, E], F32, kind="ExternalInput").ap()
    wd2 = dt("wd2", [4, 32, 128, 1024], F32, kind="ExternalInput").ap()
    vweight = dt("vweight", [NCH, 128, E], F32, kind="ExternalInput").ap()
    cosK = dt("cosK", [128, S], F32, kind="ExternalInput").ap()
    sinK = dt("sinK", [128, S], F32, kind="ExternalInput").ap()
    cosQ = dt("cosQ", [128, TOK], F32, kind="ExternalInput").ap()
    sinQ = dt("sinQ", [128, TOK], F32, kind="ExternalInput").ap()
    maskb = dt("maskb", [128, 1792], BF16, kind="ExternalInput").ap()
    identb = dt("identb", [128, 128], BF16, kind="ExternalInput").ap()
    identf = dt("identf", [128, 128], F32, kind="ExternalInput").ap()
    onesb = dt("onesb", [128, 1], BF16, kind="ExternalInput").ap()
    onesf = dt("onesf", [1, 128], F32, kind="ExternalInput").ap()
    consts = dt("consts", [128, 8], F32, kind="ExternalInput").ap()

    out = dt("out", [E, TOK], F32, kind="ExternalOutput").ap()

    fuse0 = dt("fuse0", [48 * 128, TOK], BF16, kind="Internal").ap()
    fuse1 = dt("fuse1", [64 * 128, TOK], BF16, kind="Internal").ap()
    qspill = dt("qspill", [NCH * 128, TOK], BF16, kind="Internal").ap()
    ccKin = dt("ccKin", [E, TOK], BF16, kind="Internal").ap()
    qkvw_b = dt("qkvw_b", [32, 128, E], BF16, kind="Internal").ap()
    vw_b = dt("vw_b", [NCH, 128, E], BF16, kind="Internal").ap()
    ccKout = dt("ccKout", [2 * E, TOK], BF16, kind="Internal").ap()
    ccVin = dt("ccVin", [TOK, E], BF16, kind="Internal").ap()
    ccVout = dt("ccVout", [2 * TOK, E], BF16, kind="Internal").ap()
    RG = [[0, 1], [2, 3], [4, 5], [6, 7]]

    ctx = ExitStack()
    e = ctx.enter_context
    AF32 = e(nc.sbuf_tensor("AF32", [128, 84 * KB // 4], F32))
    ABF = e(nc.sbuf_tensor("ABF", [128, 96 * KB // 2], BF16))
    AFP8 = e(nc.sbuf_tensor("AFP8", [128, 4 * KB], FP8))
    AI8 = e(nc.sbuf_tensor("AI8", [128, 2 * KB], I8))
    CST = e(nc.sbuf_tensor("CST", [128, 256], F32))
    CSTB = e(nc.sbuf_tensor("CSTB", [128, 2048], BF16))
    CSTF = e(nc.sbuf_tensor("CSTF", [128, 1024 // 4], F32))

    segf = Seg("f32", 84 * KB)
    segb = Seg("bf16", 96 * KB)

    def f32(name, cols):
        off = segf.alloc(name, cols * 4)
        return AF32[:, off // 4: off // 4 + cols]

    def bf(name, cols):
        off = segb.alloc(name, cols * 2)
        return ABF[:, off // 2: off // 2 + cols]

    psS = e(nc.psum_tensor("psS", [128, 2048], F32))
    psA = e(nc.psum_tensor("psA", [128, 1024], F32))
    psX6 = e(nc.psum_tensor("psX6", [128, 512], F32))
    psO = e(nc.psum_tensor("psO", [128, 512], F32))

    sT = Sem(e(nc.semaphore("sT")))
    sV = Sem(e(nc.semaphore("sV")))
    sS = Sem(e(nc.semaphore("sS")))
    sG = Sem(e(nc.semaphore("sG")))
    def dsem(name):
        return Sem(e(nc.semaphore(name)))
    sCst = dsem("sCst")
    sKtab = dsem("sKtab")
    sGrel = dsem("sGrel")
    sXb = [dsem("sXb0"), dsem("sXb1")]
    sWkv = dsem("sWkv")
    sRb = dsem("sRb")
    sWq = dsem("sWq")
    sPc = dsem("sPc")
    sSwap = dsem("sSwap")
    sRes = [dsem("sRes0"), dsem("sRes1")]
    sWo = [dsem("sWo0"), dsem("sWo1")]
    sO = dsem("sO")

    p = Prog(nc)
    E_ = p.emit

    def dma(eng_name, dst, src, sem):
        tok = sem.bump(16)
        E_(eng_name, lambda E, d=dst, s=src, h=sem.h: E.dma_start(out=d, in_=s)
           .then_inc(h, 16))
        return tok

    def wait(eng_name, sem, val):
        E_(eng_name, lambda E, h=sem.h, v=val: E.wait_ge(h, v))

    def mark(eng_name, instr_fn, sem, k=1):
        if eng_name in ("vector", "scalar", "gpsimd") and sem.n > 0:
            E_(eng_name, lambda E, h=sem.h, v=sem.n: E.wait_ge(h, v))
        tok = sem.bump(k)
        E_(eng_name, lambda E, f=instr_fn, h=sem.h, kk=k: f(E).then_inc(h, kk))
        return tok

    def emit(eng_name, instr_fn):
        E_(eng_name, instr_fn)

    c_cosQ = f32("cosQ", TOK)
    c_sinQ = f32("sinQ", TOK)
    t_const = 0
    t_const = dma("sync", CST[:, 0:8], consts, sCst)
    t_const = dma("sync", CST[:, 8:56], gsc, sCst)
    t_const = dma("sync", CST[:, 56:104], usc, sCst)
    t_const = dma("sync", CST[:, 104:120], dsc, sCst)
    gammaap = CST[:, 120:136]
    betaap = CST[:, 136:152]
    t_const = dma("sync", CST[:, 120:152], gammab, sCst)
    t_const = dma("sync", CSTB[:, 0:128], identb, sCst)
    t_const = dma("sync", CSTB[:, 128:129], onesb, sCst)
    t_const = dma("sync", CSTB[:, 256:2048], maskb, sCst)
    t_const = dma("sync", CSTF[:, 0:128], identf, sCst)
    t_const = dma("sync", CSTF[0:1, 128:256], onesf, sCst)
    t_const = dma("sync", c_cosQ, cosQ, sCst)
    t_const = dma("sync", c_sinQ, sinQ, sCst)
    CONST_TOK = t_const
    identb_ap = CSTB[:, 0:128]
    onesb_ap = CSTB[:, 128:129]
    maskb256 = CSTB[:, 256:512]
    maskb512 = CSTB[:, 512:1024]
    maskE = CSTB[:, 1024:1536]
    maskO = CSTB[:, 1536:2048]
    identf_ap = CSTF[:, 0:128]
    onesf_ap = CSTF[0:1, 128:256]
    c0 = lambda j: CST[:, j:j + 1]
    c0r = lambda j: CST[0:1, j:j + 1]
    gsc_col = lambda m: CST[:, 8 + m:9 + m]
    usc_col = lambda m: CST[:, 56 + m:57 + m]
    dsc_col = lambda m: CST[:, 104 + m:105 + m]

    for en in ("tensor", "vector", "scalar", "gpsimd"):
        wait(en, sCst, CONST_TOK)

    R = [None] * NCH
    Rb = [None] * NCH

    def alloc_R():
        for c in range(NCH):
            R[c] = f32(f"R{c}", TOK)

    def alloc_Rb():
        for c in range(NCH):
            Rb[c] = bf(f"Rb{c}", TOK)

    def post_ln(eps_col, src_chunks, dst_chunks, dstb_chunks=None,
                gamma_col=None, beta_col=None):
        tb = [bf("ln_tb", TOK), bf("ln_tb2", TOK)]
        sq = [bf("ln_sq", TOK), bf("ln_sq2", TOK)]
        cast_toks = []
        mm_tok = None
        for c in range(NCH):
            tcast = mark("scalar", lambda E, s=src_chunks[c], d=tb[c % 2]:
                         E.copy(out=d, in_=s), sS)
            tsq = mark("scalar", lambda E, s=src_chunks[c], d=sq[c % 2]:
                       E.activation(out=d, in_=s, func=AF.Square), sS)
            wait("tensor", sS, tsq)
            for n in range(2):
                last = (c == NCH - 1)
                mm_tok = mark("tensor", lambda E, d=psS[0:1, 0:0], c_=c, n_=n,
                              t=tb[c % 2], last_=last:
                              E.matmul(out=psS[0:1, n_ * 512:(n_ + 1) * 512],
                                       lhsT=onesb_ap,
                                       rhs=t[:, n_ * 512:(n_ + 1) * 512],
                                       start=(c_ == 0), stop=last_), sT)
                mm_tok = mark("tensor", lambda E, c_=c, n_=n, q=sq[c % 2],
                              last_=last:
                              E.matmul(out=psS[0:1, 1024 + n_ * 512:1024 + (n_ + 1) * 512],
                                       lhsT=onesb_ap,
                                       rhs=q[:, n_ * 512:(n_ + 1) * 512],
                                       start=(c_ == 0), stop=last_), sT)
            if c >= 1:
                wait("scalar", sT, mm_tok)
        meanB = f32("ln_meanB", TOK)
        invB = f32("ln_invB", TOK)
        mean_r = meanB[0:1, 0:TOK]
        var_r = invB[0:1, 0:TOK]
        inv_r = var_r
        wait("scalar", sT, mm_tok)
        t1 = mark("scalar", lambda E: E.mul(out=mean_r, in_=psS[0:1, 0:1024],
                                            mul=c0r(0)), sS)
        t1 = mark("scalar", lambda E: E.activation(out=var_r, in_=mean_r,
                                                   func=AF.Square), sS)
        wait("vector", sS, t1)
        t2 = mark("vector", lambda E: E.scalar_tensor_tensor(
            out=var_r, in0=psS[0:1, 1024:2048], scalar=c0r(0), in1=var_r,
            op0=ALU.mult, op1=ALU.subtract), sV)
        wait("scalar", sV, t2)
        t3 = mark("scalar", lambda E: E.activation(out=inv_r, in_=var_r,
                                                   func=AF.Ln, bias=eps_col), sS)
        t3 = mark("scalar", lambda E: E.activation(out=inv_r, in_=inv_r,
                                                   func=AF.Exp, scale=c0r(3)), sS)
        wait("tensor", sS, t3)
        tb_tok = None
        for n in range(2):
            tb_tok = mark("tensor", lambda E, n_=n: E.matmul(
                out=psS[:, n_ * 512:(n_ + 1) * 512], lhsT=onesf_ap,
                rhs=mean_r[0:1, n_ * 512:(n_ + 1) * 512],
                start=True, stop=True), sT)
            tb_tok = mark("tensor", lambda E, n_=n: E.matmul(
                out=psS[:, 1024 + n_ * 512:1024 + (n_ + 1) * 512], lhsT=onesf_ap,
                rhs=inv_r[0:1, n_ * 512:(n_ + 1) * 512],
                start=True, stop=True), sT)
        wait("scalar", sT, tb_tok)
        t4 = mark("scalar", lambda E: E.copy(out=meanB, in_=psS[:, 0:1024]), sS)
        t4 = mark("scalar", lambda E: E.copy(out=invB, in_=psS[:, 1024:2048]), sS)
        scratch = f32("ln_scr", TOK)
        last_v = None
        last_s = None
        for c in range(NCH):
            wait("vector", sS, t4)
            if last_s is not None:
                wait("vector", sS, last_s)
            last_v = mark("vector", lambda E, s=src_chunks[c]:
                          E.tensor_sub(out=scratch, in0=s, in1=meanB), sV)
            if gamma_col is None:
                last_v = mark("vector", lambda E, d=dst_chunks[c]:
                              E.tensor_mul(out=d, in0=scratch, in1=invB), sV)
                if dstb_chunks is not None:
                    wait("scalar", sV, last_v)
                    last_s = mark("scalar", lambda E, d=dstb_chunks[c],
                                  s=dst_chunks[c]: E.copy(out=d, in_=s), sS)
            else:
                last_v = mark("vector", lambda E:
                              E.tensor_mul(out=scratch, in0=scratch, in1=invB), sV)
                wait("scalar", sV, last_v)
                last_s = mark("scalar", lambda E, d=dstb_chunks[c],
                              g=gamma_col(c), bcol=beta_col(c):
                              E.activation(out=d, in_=scratch, func=AF.Identity,
                                           bias=bcol, scale=g), sS)
        for nm in ("ln_tb", "ln_tb2", "ln_sq", "ln_sq2"):
            segb.release(nm)
        for nm in ("ln_meanB", "ln_invB", "ln_scr"):
            segf.release(nm)
        return last_v, last_s

    kp_feat = f32("kp_feat", S)
    vp_feat = f32("vp_feat", S)
    kvg_sb = f32("kvg_sb", S)
    xb = [bf("xb0", S), bf("xb1", S)]
    wkv_b = bf("wkv_b", NCH * 256)
    wkvg_b = bf("wkvg_b", NCH * 64)
    t_wkv0 = dma("gpsimd", wkv_b[:, 0:E], wkv[0], sWkv)
    t_wkv1 = dma("gpsimd", wkv_b[:, E:2 * E], wkv[1], sWkv)
    t_wkvg = dma("gpsimd", wkvg_b, wkvg, sWkv)

    def xstream_pass(tag, lhsT_w, out_psum_slices, prev_gate=None):
        if prev_gate is not None:
            wait("gpsimd", sT, prev_gate)
        toks = []
        for c in range(NCH):
            if c >= 2:
                wait("gpsimd", sT, toks[c - 2])
            tok_x = dma("gpsimd", xb[c % 2], x0T[c * 128:(c + 1) * 128, :], sXb[c % 2])
            wait("tensor", sXb[c % 2], tok_x)
            for n in range(4):
                tok = mark("tensor", lambda E, c_=c, n_=n: E.matmul(
                    out=out_psum_slices[n_], lhsT=lhsT_w(c_),
                    rhs=xb[c_ % 2][:, n_ * 512:(n_ + 1) * 512],
                    start=(c_ == 0), stop=(c_ == NCH - 1)), sT)
            toks.append(tok)
        return toks[-1]

    wait("tensor", sWkv, t_wkvg)
    tok_kp = xstream_pass(
        "kp", lambda c: wkv_b[:, c * 128:(c + 1) * 128],
        [psS[:, n * 512:(n + 1) * 512] for n in range(4)])
    wait("scalar", sT, tok_kp)
    tkp_ev = mark("scalar", lambda E: E.copy(out=kp_feat, in_=psS[:, 0:2048]), sS)

    wait("tensor", sS, tkp_ev)
    tok_vp = xstream_pass(
        "vp", lambda c: wkv_b[:, E + c * 128:E + (c + 1) * 128],
        [psS[:, n * 512:(n + 1) * 512] for n in range(4)], prev_gate=tok_kp)
    wait("scalar", sT, tok_vp)
    tvp_ev = mark("scalar", lambda E: E.copy(out=vp_feat, in_=psS[:, 0:2048]), sS)

    wait("tensor", sS, tvp_ev)
    wait("tensor", sWkv, t_wkvg)
    tok_kvg = xstream_pass(
        "kvg", lambda c: wkvg_b[:, c * 64:(c + 1) * 64],
        [psS[0:64, n * 512:(n + 1) * 512] for n in range(4)], prev_gate=tok_vp)
    wait("scalar", sT, tok_kvg)
    tkvg_ev = mark("scalar", lambda E: E.copy(out=kvg_sb[0:64, :],
                                              in_=psS[0:64, 0:2048]), sS)
    segb.release("xb0")
    segb.release("xb1")

    c_cosK = f32("cosK", S)
    c_sinK = f32("sinK", S)
    tck = dma("sync", c_cosK, cosK, sKtab)
    tck = dma("sync", c_sinK, sinK, sKtab)
    kswap = f32("kswap", S)
    wait("sync", sS, tkp_ev)
    tsw = dma("sync", kswap[0:64, :], kp_feat[64:128, :], sSwap)
    tsw = dma("sync", kswap[64:128, :], kp_feat[0:64, :], sSwap)
    kp_rot = f32("kp_rot", S)
    wait("vector", sSwap, tsw)
    wait("vector", sKtab, tck)
    tv = mark("vector", lambda E: E.tensor_mul(out=kp_rot, in0=kp_feat, in1=c_cosK), sV)
    tv = mark("vector", lambda E: E.tensor_mul(out=kswap, in0=kswap, in1=c_sinK), sV)
    tv = mark("vector", lambda E: E.tensor_add(out=kp_rot, in0=kp_rot, in1=kswap), sV)
    segf.release("cosK")
    segf.release("sinK")
    segf.release("kp_feat")
    segf.release("kswap")

    kp_rot_tok = f32("kp_rot_tok", S)
    vp_tok = f32("vp_tok", S)
    tr_tok = None
    for (src, dst, gate_sem, gate_tok) in (
            (kp_rot, kp_rot_tok, sV, tv), (vp_feat, vp_tok, sS, tvp_ev)):
        for g in range(4):
            wait("tensor", gate_sem, gate_tok)
            for j in range(4):
                t_i = g * 4 + j
                tr_tok = mark("tensor", lambda E, s=src, t=t_i, j_=j: E.transpose(
                    out=psA[:, j_ * 128:(j_ + 1) * 128],
                    in_=s[:, t * 128:(t + 1) * 128], identity=identf_ap), sT)
            wait("scalar", sT, tr_tok)
            tr_ev = mark("scalar", lambda E, d=dst, g_=g: E.copy(
                out=d[:, g_ * 512:(g_ + 1) * 512], in_=psA[:, 0:512]), sS)
            wait("tensor", sS, tr_ev)
    segf.release("kp_rot")
    segf.release("vp_feat")

    pipe = f32("pipe", 2 * S)
    g_lo = pipe[0:32, 0:S]
    Cx = pipe[0:32, S:2 * S]
    kvh_rows = kvg_sb[0:32, :]
    wait("sync", sS, tkvg_ev)
    wait("sync", sV, tv)
    t_grel = dma("sync", g_lo, kvg_sb[32:64, :], sGrel)
    wait("scalar", sGrel, t_grel)
    ts_ = mark("scalar", lambda E: E.activation(out=g_lo, in_=g_lo, func=AF.Sigmoid), sS)
    ts_ = mark("scalar", lambda E: E.activation(out=g_lo, in_=g_lo, func=AF.Ln), sS)
    wait("vector", sS, ts_)
    tv_ = mark("vector", lambda E: E.tensor_tensor_scan(
        out=Cx, data0=g_lo, data1=g_lo, initial=0.0, op0=ALU.add, op1=ALU.bypass), sV)
    tv_ = mark("vector", lambda E: E.tensor_sub(out=Cx, in0=Cx, in1=g_lo), sV)
    wait("scalar", sV, tv_)
    ts_ = mark("scalar", lambda E: E.activation(out=Cx, in_=Cx, func=AF.Exp), sS)
    wait("vector", sS, ts_)
    tv_ = mark("vector", lambda E: E.tensor_mul(out=g_lo, in0=kvh_rows, in1=Cx), sV)
    tv_ = mark("vector", lambda E: E.tensor_tensor_scan(
        out=kvh_rows, data0=g_lo, data1=g_lo, initial=0.0,
        op0=ALU.add, op1=ALU.bypass), sV)
    kvh_s = kvh_rows
    khvh_tok = f32("khvh_tok", NCH * 32)
    tre = None
    for g in range(4):
        wait("tensor", sV, tv_)
        if tre is not None:
            wait("tensor", sS, tre)
        trt = None
        for j in range(4):
            t_i = g * 4 + j
            trt = mark("tensor", lambda E, t=t_i, j_=j: E.transpose(
                out=psA[0:128, j_ * 128:j_ * 128 + 32],
                in_=kvh_s[:, t * 128:(t + 1) * 128],
                identity=identf_ap[0:32, 0:32]), sT)
        wait("scalar", sT, trt)
        for j in range(4):
            t_i = g * 4 + j
            tre = mark("scalar", lambda E, t=t_i, j_=j: E.copy(
                out=khvh_tok[:, t * 32:(t + 1) * 32],
                in_=psA[:, j_ * 128:j_ * 128 + 32]), sS)
    wait("tensor", sS, tre)
    segf.release("pipe")
    segf.release("kvg_sb")

    kh_col = lambda tile, h: khvh_tok[:, tile * 32 + h:tile * 32 + h + 1]
    vh_col = lambda tile, h: khvh_tok[:, tile * 32 + H + h:tile * 32 + H + h + 1]

    alloc_Rb()
    wait("gpsimd", sT, tok_kvg)
    t_rb = None
    for c in range(NCH):
        t_rb = dma("gpsimd", Rb[c], x0To[c * 128:(c + 1) * 128, :], sRb)
    RB_TOK = t_rb
    segb.release("wkv_b")
    segb.release("wkvg_b")

    O_feat = bf("O_feat", NCH * TOK)
    PT_buf = bf("PT_buf", NCH * 256)
    Kp_f = bf("Kpf", S)
    Vp_t = bf("Vpt", S)
    Kp_t = bf("Kpt", S)
    qfeat = f32("qfeat", TOK)
    qswap = f32("qswap", TOK)
    qrot = bf("qrot", TOK)
    Pf = bf("Pf", S)
    cols = f32("cols", 16)
    negmax = cols[:, 0:2]
    mh = cols[:, 2:4]
    rowsum = cols[:, 4:6]
    rinv = cols[:, 6:8]
    diagb = bf("diag", 256)
    wqb = [bf("wqb0", NCH * 128)]

    wq_cons = {}
    psX_free = None
    psO_free = None
    psA_free = None
    ofeat_done = [None] * H
    prev_head_t_done = None
    prev_tsh = None
    prev_tex = None
    prev_pt_last = None
    prev_grp_pv = None

    wait("gpsimd", sCst, CONST_TOK)
    for h in range(H):
        if h >= 1:
            wait("gpsimd", sT, wq_cons[h - 1])
        t_wq = dma("gpsimd", wqb[0], wq[h], sWq)
        for j_ in range(3 * h, 3 * h + 3):
            if j_ < 32:
                t_pc = dma("gpsimd", qkvw_b[j_], qkvw[j_], sPc)
            else:
                t_pc = dma("gpsimd", vw_b[j_ - 32], vweight[j_ - 32], sPc)
        wait("tensor", sWq, t_wq)
        wait("tensor", sRb, RB_TOK)
        if psA_free is not None:
            wait("tensor", sS, psA_free)
        if psX_free is not None:
            wait("tensor", sS, psX_free)
        tq = None
        for c in range(NCH):
            for n in range(2):
                tq = mark("tensor", lambda E, c_=c, n_=n: E.matmul(
                    out=psA[:, n_ * 512:(n_ + 1) * 512],
                    lhsT=wqb[0][:, c_ * 128:(c_ + 1) * 128],
                    rhs=Rb[c_][:, n_ * 512:(n_ + 1) * 512],
                    start=(c_ == 0), stop=(c_ == NCH - 1)), sT)
        wq_cons[h] = tq
        wait("scalar", sT, tq)
        if h > 0:
            wait("scalar", sV, head_tqv)
        tqe = mark("scalar", lambda E: E.copy(out=qfeat, in_=psA[:]), sS)
        psA_free = tqe
        if prev_head_t_done is not None:
            wait("vector", sT, prev_head_t_done)
        if h == 0:
            wait("vector", sS, tre)
        tkv = None
        for t_i in range(NCH):
            tkv = mark("vector", lambda E, t=t_i, hh=h: E.tensor_scalar(
                out=Kp_t[:, t * 128:(t + 1) * 128],
                in0=kp_rot_tok[:, t * 128:(t + 1) * 128],
                scalar1=kh_col(t, hh), scalar2=None, op0=ALU.mult), sV)
            tkv = mark("vector", lambda E, t=t_i, hh=h: E.tensor_scalar(
                out=Vp_t[:, t * 128:(t + 1) * 128],
                in0=vp_tok[:, t * 128:(t + 1) * 128],
                scalar1=vh_col(t, hh), scalar2=None, op0=ALU.mult), sV)
        wait("sync", sS, tqe)
        if h > 0:
            wait("sync", sV, head_tqv)
        tqs = dma("sync", qswap[0:64, :], qfeat[64:128, :], sSwap)
        tqs = dma("sync", qswap[64:128, :], qfeat[0:64, :], sSwap)
        wait("vector", sSwap, tqs)
        tqv = mark("vector", lambda E: E.tensor_mul(out=qfeat, in0=qfeat, in1=c_cosQ), sV)
        tqv = mark("vector", lambda E: E.tensor_mul(out=qswap, in0=qswap, in1=c_sinQ), sV)
        tqv = mark("vector", lambda E: E.tensor_add(out=qrot, in0=qfeat, in1=qswap), sV)
        head_tqv = tqv
        wait("tensor", sS, tqe)
        for g in range(2):
            wait("tensor", sV, tkv)
            if psX_free is not None:
                wait("tensor", sS, psX_free)
            trt = None
            for j in range(8):
                t_i = g * 8 + j
                trt = mark("tensor", lambda E, t=t_i, j_=j: E.matmul(
                    out=psA[:, j_ * 128:(j_ + 1) * 128],
                    lhsT=Kp_t[:, t * 128:(t + 1) * 128], rhs=identb_ap,
                    start=True, stop=True), sT)
            wait("scalar", sT, trt)
            psX_free = mark("scalar", lambda E, g_=g: E.copy(
                out=Kp_f[:, g_ * 1024:(g_ + 1) * 1024], in_=psA[:]), sS)
        tKpf = psX_free

        def slices_of(keys):
            out, off = [], 0
            while off < keys:
                w = min(512, keys - off)
                out.append((off, w))
                off += w
            return out

        def emit_pt(qt_, diag_tok):
            nonlocal psX_free, prev_pt_last
            keys = 256 * (qt_ + 1)
            ntile = keys // 128
            dg = diagb[:, (qt_ % 2) * 128:(qt_ % 2) * 128 + 128]
            src_p = Pfp[qt_ % 2]
            trt = None
            for g0 in range(0, ntile, 8):
                gn = min(8, ntile - g0)
                wait("tensor", sV, diag_tok)
                if psX_free is not None:
                    wait("tensor", sS, psX_free)
                for j in range(gn):
                    kc = g0 + j
                    trt = mark("tensor", lambda E, k_=kc, j_=j, d=dg, sp=src_p:
                               E.matmul(
                        out=psA[:, j_ * 128:(j_ + 1) * 128],
                        lhsT=sp[:, k_ * 128:(k_ + 1) * 128], rhs=d,
                        start=True, stop=True), sT)
                wait("scalar", sT, trt)
                dst = PT_buf[:, (qt_ % 2) * 2048 + g0 * 128:
                             (qt_ % 2) * 2048 + (g0 + gn) * 128]
                psX_free = mark("scalar", lambda E, d=dst, g_=gn: E.copy(
                    out=d, in_=psA[:, 0:g_ * 128]), sS)
            prev_pt_last = trt

        def emit_pv(qt_, hh):
            nonlocal psO_free, prev_grp_pv
            keys = 256 * (qt_ + 1)
            ntile = keys // 128
            wait("tensor", sS, psX_free)
            if qt_ % 4 == 0 and psO_free is not None:
                wait("tensor", sS, psO_free)
            tpv = None
            for kc in range(ntile):
                tpv = mark("tensor", lambda E, k_=kc, q_=qt_: E.matmul(
                    out=psO[:, (q_ % 4) * 128:(q_ % 4) * 128 + 128],
                    lhsT=Vp_t[:, k_ * 128:(k_ + 1) * 128],
                    rhs=PT_buf[:, (q_ % 2) * 2048 + k_ * 128:
                               (q_ % 2) * 2048 + (k_ + 1) * 128],
                    start=(k_ == 0), stop=(k_ == ntile - 1)), sT)
            prev_grp_pv = tpv
            if qt_ % 4 == 3:
                wait("scalar", sT, tpv)
                psO_free = mark("scalar", lambda E, hh_=hh, g_=qt_ // 4: E.copy(
                    out=O_feat[:, hh_ * 1024 + g_ * 512:hh_ * 1024 + (g_ + 1) * 512],
                    in_=psO[:]), sS)

        Pfp = [Pf, Kp_t]
        pend_pt = None
        pend_exp = None
        shift_toks = None
        for qt in range(8):
            keys = 256 * (qt + 1)
            sls = slices_of(keys)
            wait("tensor", sV, tqv)
            wait("tensor", sS, tKpf)
            tqk = None
            for i, (off, w) in enumerate(sls):
                if shift_toks is not None and i < len(shift_toks):
                    wait("tensor", sS, shift_toks[min(i, len(shift_toks) - 1)])
                last = (i == len(sls) - 1)
                tqk = mark("tensor", lambda E, o_=off, w_=w, q_=qt,
                           sp_=not last: E.matmul(
                    out=psS[:, o_:o_ + w_],
                    lhsT=qrot[:, q_ * 128:(q_ + 1) * 128],
                    rhs=Kp_f[:, o_:o_ + w_],
                    start=True, stop=sp_), sT)
                if last:
                    m_ap = maskb256 if w == 256 else maskb512
                    tqk = mark("tensor", lambda E, o_=off, w_=w, m=m_ap: E.matmul(
                        out=psS[:, o_:o_ + w_], lhsT=identb_ap, rhs=m,
                        start=False, stop=True), sT)
            if pend_pt is not None:
                emit_pt(*pend_pt)
                emit_pv(pend_pt[0], h)
                pend_pt = None
            pm = qt % 2
            nm = negmax[:, pm:pm + 1]
            wait("vector", sT, tqk)
            if shift_toks is not None:
                wait("vector", sS, shift_toks[-1])
            tcmb = mark("vector", lambda E, k_=keys, d=nm: E.reduce_max(
                out=d, in_=psS[:, 0:k_], axis=AX.X, negate=True), sV)
            if pend_exp is not None:
                q_, tex_ = pend_exp
                wait("vector", sS, tex_)
                mark("vector", lambda E, q2=q_ % 2: E.reciprocal(
                    out=rinv[:, q2:q2 + 1], in_=rowsum[:, q2:q2 + 1]), sV)
                if prev_pt_last is not None:
                    wait("vector", sT, prev_pt_last)
                tdg = mark("vector", lambda E, q2=q_ % 2: E.tensor_scalar(
                    out=diagb[:, q2 * 128:(q2 + 1) * 128], in0=identb_ap,
                    scalar1=rinv[:, q2:q2 + 1], scalar2=None, op0=ALU.mult), sV)
                pend_pt = (q_, tdg)
                pend_exp = None
            wait("scalar", sV, tcmb)
            if prev_pt_last is not None:
                wait("scalar", sT, prev_pt_last)
            shift_toks = []
            for (off, w) in sls:
                tshn = mark("scalar", lambda E, o_=off, w_=w, b=nm, q2=pm:
                            E.activation(
                    out=Pfp[q2][:, o_:o_ + w_], in_=psS[:, o_:o_ + w_],
                    func=AF.Identity, bias=b), sS)
                shift_toks.append(tshn)
            prev_tsh = shift_toks[-1]
            tex = mark("scalar", lambda E, q2=pm, k_=keys: E.activation(
                out=Pfp[q2][:, 0:k_], in_=Pfp[q2][:, 0:k_], func=AF.Exp,
                accum_out=rowsum[:, q2:q2 + 1]), sS)
            pend_exp = (qt, tex)
        if pend_pt is not None:
            emit_pt(*pend_pt)
            emit_pv(pend_pt[0], h)
        q_, tex_ = pend_exp
        wait("vector", sS, tex_)
        mark("vector", lambda E, q2=q_ % 2: E.reciprocal(
            out=rinv[:, q2:q2 + 1], in_=rowsum[:, q2:q2 + 1]), sV)
        if prev_pt_last is not None:
            wait("vector", sT, prev_pt_last)
        tdg = mark("vector", lambda E, q2=q_ % 2: E.tensor_scalar(
            out=diagb[:, q2 * 128:(q2 + 1) * 128], in0=identb_ap,
            scalar1=rinv[:, q2:q2 + 1], scalar2=None, op0=ALU.mult), sV)
        emit_pt(q_, tdg)
        emit_pv(q_, h)
        pend_exp = None
        ofeat_done[h] = psO_free
        prev_head_t_done = prev_grp_pv

    segf.release("qfeat")
    segf.release("qswap")

    segb.release("Pf")
    segf.release("kp_rot_tok")
    segf.release("vp_tok")
    segf.release("khvh_tok")
    segb.release("Kpf")
    segb.release("Vpt")
    segb.release("Kpt")
    segb.release("qrot")
    segb.release("PT_buf")
    segf.release("cosQ")
    segf.release("sinQ")
    segf.release("cols")
    alloc_R()
    woutb = [bf("wob0", NCH * 128), bf("wob1", NCH * 128)]
    rres = [f32("rres0", TOK), f32("rres1", TOK)]
    wout_cons = {}
    rres_cons = {}
    wait("tensor", sS, ofeat_done[H - 1])
    wait("gpsimd", sT, prev_head_t_done)
    wait("sync", sT, prev_head_t_done)
    ev_tok = None
    for mt in range(NCH):
        if mt >= 2:
            wait("gpsimd", sT, wout_cons[mt - 2])
            wait("sync", sV, rres_cons[mt - 2])
        t_w = dma("gpsimd", woutb[mt % 2], wout[mt], sWo[mt % 2])
        t_r = dma("sync", rres[mt % 2], x0To[mt * 128:(mt + 1) * 128, :], sRes[mt % 2])
        wait("tensor", sWo[mt % 2], t_w)
        if psA_free is not None:
            wait("tensor", sV if mt > 0 else sS, psA_free)
        tmm = None
        for kc in range(NCH):
            for n in range(2):
                tmm = mark("tensor", lambda E, k_=kc, n_=n, m_=mt: E.matmul(
                    out=psA[:, n_ * 512:(n_ + 1) * 512],
                    lhsT=woutb[m_ % 2][:, k_ * 128:(k_ + 1) * 128],
                    rhs=O_feat[:, k_ * 1024 + n_ * 512:k_ * 1024 + (n_ + 1) * 512],
                    start=(k_ == 0), stop=(k_ == NCH - 1)), sT)
        wout_cons[mt] = tmm
        wait("vector", sT, tmm)
        wait("vector", sRes[mt % 2], t_r)
        ev_tok = mark("vector", lambda E, m_=mt: E.scalar_tensor_tensor(
            out=R[m_], in0=psA[:], scalar=1.0, in1=rres[m_ % 2],
            op0=ALU.mult, op1=ALU.add), sV)
        rres_cons[mt] = ev_tok
        psA_free = ev_tok
    segb.release("O_feat")
    segb.release("wob0")
    segb.release("wob1")
    segf.release("rres0")
    segf.release("rres1")
    segb.release("wqb0")
    wait("scalar", sV, ev_tok)
    lnv, lns = post_ln(c0r(1), R, R, dstb_chunks=Rb)

    def finish():
        tdone = None
        wait("sync", sS, sS.n)
        wait("sync", sV, sV.n)
        for c in range(NCH):
            tdone = dma("sync", out[c * 128:(c + 1) * 128, :], R[c], sO)
        wait("sync", sO, tdone)
        p_finalize(nc, p, ctx)
        return nc

    if dbg_phase == "A":
        return finish()

    sWst = dsem("sWst")
    sDst = dsem("sDst")
    sFz = [dsem("sFz0"), dsem("sFz1")]
    sFr = [dsem("sFr0"), dsem("sFr1")]
    sDw = [dsem("sDw0"), dsem("sDw1")]

    wstage = f32("wstage", S)
    scr1 = f32("b_scr1", TOK)
    scr2 = f32("b_scr2", TOK)
    w8a = AFP8[:, 0:2048]
    w8b = AFP8[:, 2048:4096]
    gb = [bf("gb0", S), bf("gb1", S)]
    ub = [bf("ub0", S), bf("ub1", S)]
    fzb = [bf("fz0", TOK), bf("fz1", TOK)]

    wait("gpsimd", sS, lns)
    g_cons = {}
    u_cons = {}
    fz_cons = {}
    prev_silu = None
    prev_ucp = None
    prev_v8 = {}
    prev_s8 = {}
    prev_tf = None
    silu_par = [None, None]
    ucp_par = [None, None]
    gtgt = [[psA[:, 0:512], psA[:, 512:1024]], [psX6[:], psO[:]]]
    utgt = [psS[:, 0:1024], psS[:, 1024:2048]]

    def stage_pair(m):
        if m >= 1:
            wait("gpsimd", sV, prev_v8[(m - 1, "u")])
        t_g = dma("gpsimd", wstage, gatew[m], sWst)
        wait("vector", sWst, t_g)
        if m >= 1:
            wait("vector", sS, prev_s8[(m - 1, "g")])
        v8g = mark("vector", lambda E: E.tensor_copy(out=w8a, in_=wstage), sV)
        prev_v8[(m, "g")] = v8g
        if m >= 2 and (m - 2) in g_cons:
            wait("scalar", sT, g_cons[m - 2])
        wait("scalar", sV, v8g)
        s8g = mark("scalar", lambda E, m_=m: E.copy(out=gb[m_ % 2], in_=w8a), sS)
        prev_s8[(m, "g")] = s8g
        wait("gpsimd", sV, v8g)
        t_u = dma("gpsimd", wstage, upw[m], sWst)
        wait("vector", sWst, t_u)
        if m >= 1:
            wait("vector", sS, prev_s8[(m - 1, "u")])
        v8u = mark("vector", lambda E: E.tensor_copy(out=w8b, in_=wstage), sV)
        prev_v8[(m, "u")] = v8u
        if m >= 2 and (m - 2) in u_cons:
            wait("scalar", sT, u_cons[m - 2])
        wait("scalar", sV, v8u)
        s8u = mark("scalar", lambda E, m_=m: E.copy(out=ub[m_ % 2], in_=w8b), sS)
        prev_s8[(m, "u")] = s8u
        return s8g, s8u

    staged = {0: stage_pair(0)}
    for mt in range(48):
        par = mt % 2
        s8g, s8u = staged.pop(mt)
        wait("tensor", sS, s8g)
        if silu_par[par] is not None:
            wait("tensor", sS, silu_par[par])
        tg = None
        for c in range(NCH):
            for n in range(2):
                tg = mark("tensor", lambda E, c_=c, n_=n, m_=mt, p_=par:
                          E.matmul(
                    out=gtgt[p_][n_],
                    lhsT=gb[m_ % 2][:, c_ * 128:(c_ + 1) * 128],
                    rhs=Rb[c_][:, n_ * 512:(n_ + 1) * 512],
                    start=(c_ == 0), stop=(c_ == NCH - 1)), sT)
        g_cons[mt] = tg
        wait("tensor", sS, s8u)
        if ucp_par[par] is not None:
            wait("tensor", sS, ucp_par[par])
        tu = None
        for c in range(NCH):
            for n in range(2):
                tu = mark("tensor", lambda E, c_=c, n_=n, m_=mt, p_=par:
                          E.matmul(
                    out=utgt[p_][:, n_ * 512:(n_ + 1) * 512],
                    lhsT=ub[m_ % 2][:, c_ * 128:(c_ + 1) * 128],
                    rhs=Rb[c_][:, n_ * 512:(n_ + 1) * 512],
                    start=(c_ == 0), stop=(c_ == NCH - 1)), sT)
        u_cons[mt] = tu
        if mt + 1 < 48:
            staged[mt + 1] = stage_pair(mt + 1)
        wait("scalar", sT, tg)
        if prev_tf is not None:
            wait("scalar", sV, prev_tf)
        if par == 0:
            prev_silu = mark("scalar", lambda E, m_=mt: E.activation(
                out=scr1, in_=psA[:], func=AF.Silu, scale=gsc_col(m_)), sS)
        else:
            for half in range(2):
                prev_silu = mark("scalar", lambda E, m_=mt, h_=half:
                                 E.activation(
                    out=scr1[:, h_ * 512:(h_ + 1) * 512], in_=gtgt[1][h_],
                    func=AF.Silu, scale=gsc_col(m_)), sS)
        silu_par[par] = prev_silu
        wait("scalar", sT, tu)
        prev_ucp = mark("scalar", lambda E, m_=mt, p_=par: E.activation(
            out=scr2, in_=utgt[p_], func=AF.Copy, scale=usc_col(m_)), sS)
        ucp_par[par] = prev_ucp
        wait("vector", sS, prev_ucp)
        if mt >= 2:
            wait("vector", sFz[mt % 2], fz_cons[mt - 2])
        tf = mark("vector", lambda E, m_=mt: E.tensor_mul(
            out=fzb[m_ % 2], in0=scr1, in1=scr2), sV)
        prev_tf = tf
        wait("sync", sV, tf)
        fz_cons[mt] = dma("sync", fuse0[mt * 128:(mt + 1) * 128, :],
                          fzb[mt % 2], sFz[mt % 2])
    dwf = f32("dw_f", 2 * 512)
    dwb = [bf("dw0", 2 * 512), bf("dw1", 2 * 512)]
    frb = [bf("fr0", TOK), bf("fr1", TOK)]
    w8d = AFP8[:, 0:1024]
    ev_b = None
    fr_cons = {}
    dw_cons = {}
    dstage_cons = None
    prev_s8dn = None
    wait("sync", sFz[0], fz_cons[46])
    wait("sync", sFz[1], fz_cons[47])
    pairs = [(g, k) for g in range(4) for k in range(24)]

    def stage_dn(idx):
        nonlocal dstage_cons, prev_s8dn
        grp, kb = pairs[idx]
        slot = kb % 2
        if dstage_cons is not None:
            wait("gpsimd", sV, dstage_cons)
        t_d = dma("gpsimd", dwf, downw2[grp, kb], sDst)
        wait("vector", sDst, t_d)
        if prev_s8dn is not None:
            wait("vector", sS, prev_s8dn)
        dstage_cons = mark("vector", lambda E: E.tensor_copy(
            out=w8d, in_=dwf), sV)
        if idx >= 2 and pairs[idx - 2] in dw_cons:
            wait("scalar", sT, dw_cons[pairs[idx - 2]])
        wait("scalar", sV, dstage_cons)
        s8d = mark("scalar", lambda E, sl=slot: E.copy(
            out=dwb[sl], in_=w8d), sS)
        prev_s8dn = s8d
        return s8d

    staged_dn = stage_dn(0)
    dn_tgt = [[psS[:, 0:512], psS[:, 512:1024]],
              [psS[:, 1024:1536], psS[:, 1536:2048]],
              [psA[:, 0:512], psA[:, 512:1024]],
              [psX6[:], psO[:]]]
    for idx, (grp, kb) in enumerate(pairs):
        if kb == 0:
            mts = [4 * grp, 4 * grp + 1, 4 * grp + 2, 4 * grp + 3]
        slot = kb % 2
        s8d = staged_dn
        for j in range(2):
            kc = 2 * kb + j
            if (grp, kb) != (0, 0) or j > 0:
                prevk2 = (grp, kc - 2)
                if kc >= 2 and prevk2 in fr_cons:
                    wait("sync", sT, fr_cons[prevk2])
                elif kc < 2 and grp > 0 and (grp - 1, 46 + kc) in fr_cons:
                    wait("sync", sT, fr_cons[(grp - 1, 46 + kc)])
            t_f = dma("sync", frb[kc % 2],
                      fuse0[kc * 128:(kc + 1) * 128, :], sFr[kc % 2])
            wait("tensor", sS, s8d)
            wait("tensor", sFr[kc % 2], t_f)
            if kc == 0 and ev_b is not None:
                wait("tensor", sV, ev_b)
            tt = None
            for mi, m in enumerate(mts):
                for n in range(2):
                    tt = mark("tensor", lambda E, mi_=mi, j_=j, n_=n,
                              k_=kc, sl=slot: E.matmul(
                        out=dn_tgt[mi_][n_],
                        lhsT=dwb[sl][:, j_ * 512 + mi_ * 128:
                                     j_ * 512 + (mi_ + 1) * 128],
                        rhs=frb[k_ % 2][:, n_ * 512:(n_ + 1) * 512],
                        start=(k_ == 0), stop=(k_ == 47)), sT)
            fr_cons[(grp, kc)] = tt
            dw_cons[(grp, kb)] = tt
        if idx + 1 < len(pairs):
            staged_dn = stage_dn(idx + 1)
        if kb == 23:
            wait("vector", sT, dw_cons[(grp, 23)])
            for mi, m in enumerate(mts):
                for n in range(2):
                    ev_b = mark("vector", lambda E, mi_=mi, m_=m, n_=n:
                                E.scalar_tensor_tensor(
                        out=R[m_][:, n_ * 512:(n_ + 1) * 512],
                        in0=dn_tgt[mi_][n_], scalar=dsc_col(m_),
                        in1=R[m_][:, n_ * 512:(n_ + 1) * 512],
                        op0=ALU.mult, op1=ALU.add), sV)
    segf.release("wstage")
    segf.release("dw_f")
    segf.release("b_scr1")
    segf.release("b_scr2")
    for nm in ("gb0", "gb1", "ub0", "ub1", "fz0", "fz1", "dw0", "dw1",
               "fr0", "fr1"):
        segb.release(nm)
    wait("scalar", sV, ev_b)
    lnv, lns = post_ln(c0r(1), R, R)

    if dbg_phase == "B":
        return finish()

    sQw = [dsem("sQw0"), dsem("sQw1")]
    sVw = [dsem("sVw0"), dsem("sVw1")]
    sKsp = [dsem("sKsp0"), dsem("sKsp1")]
    sVsp = dsem("sVsp")
    sKst = [dsem("sKst0"), dsem("sKst1")]
    sVst = [dsem("sVst0"), dsem("sVst1")]

    xn = Rb
    qbuf = bf("qbuf", NCH * TOK)
    wait("scalar", sV, lnv)
    lnv2, xn_tok = post_ln(c0r(1), R, R, dstb_chunks=xn,
                           gamma_col=lambda c: gammaap[:, c:c + 1],
                           beta_col=lambda c: betaap[:, c:c + 1])

    qkvb = [bf("qkvb0", S), bf("qkvb1", S)]
    evb = [bf("evb0", TOK), bf("evb1", TOK)]
    qkv_cons = {}
    ev_cons = {}
    prev_evs = None
    wait("gpsimd", sS, xn_tok)
    wait("sync", sS, xn_tok)
    wait("sync", sPc, t_pc)
    for mt in range(NCH, 2 * NCH):
        if mt >= NCH + 2:
            wait("sync", sT, qkv_cons[mt - 2])
        t_w = dma("sync", qkvb[mt % 2], qkvw_b[mt], sQw[mt % 2])
        wait("tensor", sQw[mt % 2], t_w)
        wait("tensor", sS, xn_tok)
        if prev_evs is not None:
            wait("tensor", sS, prev_evs)
        tmm = None
        for c in range(NCH):
            for n in range(2):
                tmm = mark("tensor", lambda E, c_=c, n_=n, m_=mt: E.matmul(
                    out=psA[:, n_ * 512:(n_ + 1) * 512],
                    lhsT=qkvb[m_ % 2][:, c_ * 128:(c_ + 1) * 128],
                    rhs=xn[c_][:, n_ * 512:(n_ + 1) * 512],
                    start=(c_ == 0), stop=(c_ == NCH - 1)), sT)
        qkv_cons[mt] = tmm
        wait("scalar", sT, tmm)
        if mt >= NCH + 2:
            wait("scalar", sKsp[mt % 2], ev_cons[mt - 2])
        prev_evs = mark("scalar", lambda E, m_=mt: E.mul(
            out=evb[m_ % 2], in_=psA[:], mul=c0(7)), sS)
        wait("scalar", sS, prev_evs)
        ev_cons[mt] = dma("scalar", ccKin[(mt - NCH) * 128:(mt - NCH + 1) * 128, :],
                          evb[mt % 2], sKsp[mt % 2])
    wait("gpsimd", sKsp[0], ev_cons[30])
    wait("gpsimd", sKsp[1], ev_cons[31])
    t_ccK = mark("gpsimd", lambda E: E.collective_compute(
        "AllGather", ALU.bypass, replica_groups=RG,
        ins=[ccKin], outs=[ccKout]), sG)
    vstage = [bf("vst0", TOK), bf("vst1", TOK)]
    vw_cons = {}
    t_vsp = 0
    prev_vev = prev_evs
    for dh in range(2):
        for tg in range(3):
            tts = [t for t in (3 * tg, 3 * tg + 1, 3 * tg + 2) if t < 8]
            pgrp = [psS[:, 0:1024], psS[:, 1024:2048], psA[:]]
            tvm = None
            for kc in range(NCH):
                sl = kc % 2
                key = (dh, tg, kc - 2)
                if key in vw_cons:
                    wait("sync", sT, vw_cons[key])
                elif (dh, tg) != (0, 0) and kc < 2:
                    pk = (dh, tg - 1, 14 + kc) if tg > 0 else (dh - 1, 2, 14 + kc)
                    if pk in vw_cons:
                        wait("sync", sT, vw_cons[pk])
                t_v = dma("sync", vstage[sl],
                          vw_b[kc, :, dh * 1024:(dh + 1) * 1024], sVw[sl])
                wait("tensor", sVw[sl], t_v)
                if kc == 0 and prev_vev is not None:
                    wait("tensor", sS, prev_vev)
                for ti, tt_ in enumerate(tts):
                    for n in range(2):
                        tvm = mark("tensor", lambda E, ti_=ti, t_=tt_, n_=n,
                                   c_=kc, sl_=sl: E.matmul(
                            out=pgrp[ti_][:, n_ * 512:(n_ + 1) * 512],
                            lhsT=xn[c_][:, t_ * 128:(t_ + 1) * 128],
                            rhs=vstage[sl_][:, n_ * 512:(n_ + 1) * 512],
                            start=(c_ == 0), stop=(c_ == NCH - 1)), sT)
                vw_cons[(dh, tg, kc)] = tvm
            for ti, tt_ in enumerate(tts):
                wait("scalar", sT, tvm)
                if (dh, tg, ti) == (0, 0, 0):
                    wait("scalar", sKsp[0], ev_cons[30])
                    wait("scalar", sKsp[1], ev_cons[31])
                else:
                    wait("scalar", sVsp, t_vsp)
                prev_vev = mark("scalar", lambda E, ti_=ti: E.copy(
                    out=evb[0], in_=pgrp[ti_]), sS)
                wait("scalar", sS, prev_vev)
                t_vsp = dma("scalar",
                            ccVin[tt_ * 128:(tt_ + 1) * 128,
                                  dh * 1024:(dh + 1) * 1024],
                            evb[0], sVsp)
    prev_evs = prev_vev
    for mt in range(NCH):
        pred = 30 if mt == 0 else (31 if mt == 1 else mt - 2)
        wait("sync", sT, qkv_cons[pred])
        t_w = dma("sync", qkvb[mt % 2], qkvw_b[mt], sQw[mt % 2])
        wait("tensor", sQw[mt % 2], t_w)
        if prev_evs is not None:
            wait("tensor", sS, prev_evs)
        tmm = None
        for c in range(NCH):
            for n in range(2):
                tmm = mark("tensor", lambda E, c_=c, n_=n, m_=mt: E.matmul(
                    out=psA[:, n_ * 512:(n_ + 1) * 512],
                    lhsT=qkvb[m_ % 2][:, c_ * 128:(c_ + 1) * 128],
                    rhs=xn[c_][:, n_ * 512:(n_ + 1) * 512],
                    start=(c_ == 0), stop=(c_ == NCH - 1)), sT)
        qkv_cons[mt] = tmm
        wait("scalar", sT, tmm)
        prev_evs = mark("scalar", lambda E, m_=mt: E.copy(
            out=qbuf[:, m_ * 1024:(m_ + 1) * 1024], in_=psA[:]), sS)
    q_last = prev_evs
    wait("gpsimd", sVsp, t_vsp)
    t_ccV = mark("gpsimd", lambda E: E.collective_compute(
        "AllGather", ALU.bypass, replica_groups=RG,
        ins=[ccVin], outs=[ccVout]), sG)

    segb.release("vst0")
    segb.release("vst1")
    segb.release("evb0")
    segb.release("evb1")
    Kst = [bf("Kst0", S), bf("Kst1", S)]
    segb.release("qkvb0")
    segb.release("qkvb1")
    for c in range(NCH):
        segb.release(f"Rb{c}")
    O1 = bf("O1", NCH * TOK)
    Vst = [bf("Vst0", S), bf("Vst1", S)]
    PT1 = bf("PT1", NCH * 128)
    Otk = bf("Otk", S)
    Sf1 = f32("Sf1", S)
    P1 = bf("P1", S)
    diag1 = bf("diag1", 128)
    cols1 = f32("cols1", 16)
    negmax1 = cols1[:, 0:1]
    rowsum1 = cols1[:, 1:2]
    rinv1 = cols1[:, 2:3]
    kst_cons = {}
    vst_cons = {}
    m1_cons = {}
    prev_sh1 = None
    prev_pt1 = None
    prev_opv = None
    o1_done = None
    wait("sync", sG, t_ccK)
    wait("tensor", sS, q_last)
    wait("gpsimd", sG, t_ccV)
    wait("gpsimd", sS, q_last)

    def emit_pt1(qt_, diag_tok, ntile):
        nonlocal psX_free, prev_pt1
        trt = None
        for g0 in range(0, ntile, 8):
            gn = min(8, ntile - g0)
            wait("tensor", sV, diag_tok)
            if psX_free is not None:
                wait("tensor", sS, psX_free)
            for j in range(gn):
                kc = g0 + j
                trt = mark("tensor", lambda E_, k_=kc, j_=j: E_.matmul(
                    out=psA[:, j_ * 128:(j_ + 1) * 128],
                    lhsT=P1[:, k_ * 128:(k_ + 1) * 128], rhs=diag1,
                    start=True, stop=True), sT)
            wait("scalar", sT, trt)
            psX_free = mark("scalar", lambda E_, g0_=g0, g_=gn: E_.copy(
                out=PT1[:, g0_ * 128:(g0_ + g_) * 128],
                in_=psA[:, 0:g_ * 128]), sS)
        prev_pt1 = trt

    def emit_pv1(qt_, ntile):
        nonlocal prev_opv, psX_free, o1_done, prev_pt1
        wait("tensor", sS, psX_free)
        tpv = None
        for kc in range(ntile):
            sl = kc % 2
            if kc >= 2:
                pred = (qt_, kc - 2)
            elif qt_ >= 1:
                np_ = 2 * qt_
                pred = (qt_ - 1, np_ - 1 if (np_ - 1) % 2 == sl else np_ - 2)
            else:
                pred = None
            if pred is not None and pred in vst_cons:
                wait("gpsimd", sT, vst_cons[pred])
            if kc <= qt_:
                row = kc * 128
            else:
                row = TOK + (kc - qt_ - 1) * 128
            t_v1 = dma("gpsimd", Vst[sl], ccVout[row:row + 128, :], sVst[sl])
            wait("tensor", sVst[sl], t_v1)
            for n in range(4):
                tpv = mark("tensor", lambda E_, c_=kc, n_=n, sl_=sl: E_.matmul(
                    out=psS[:, n_ * 512:(n_ + 1) * 512],
                    lhsT=PT1[:, c_ * 128:(c_ + 1) * 128],
                    rhs=Vst[sl_][:, n_ * 512:(n_ + 1) * 512],
                    start=(c_ == 0), stop=(c_ == ntile - 1)), sT)
            vst_cons[(qt_, kc)] = tpv
        wait("scalar", sT, tpv)
        prev_opv = mark("scalar", lambda E_: E_.copy(out=Otk, in_=psS[:]), sS)
        trt = None
        for g0 in range(0, NCH, 8):
            wait("tensor", sS, prev_opv)
            wait("tensor", sS, psX_free)
            for j in range(8):
                mt = g0 + j
                trt = mark("tensor", lambda E_, m_=mt, j_=j: E_.matmul(
                    out=psA[:, j_ * 128:(j_ + 1) * 128],
                    lhsT=Otk[:, m_ * 128:(m_ + 1) * 128], rhs=identb_ap,
                    start=True, stop=True), sT)
            wait("scalar", sT, trt)
            dst = O1[:, qt_ * 2048 + g0 * 128:qt_ * 2048 + (g0 + 8) * 128]
            psX_free = mark("scalar", lambda E_, d=dst: E_.copy(
                out=d, in_=psA[:]), sS)
        o1_done = psX_free
        prev_pt1 = trt

    for qt in range(8):
        eoff = 128 * (qt + 1)
        keys = 2 * eoff
        ntile = keys // 128
        sls = []
        soff = 0
        while soff < keys:
            w = min(512, keys - soff)
            sls.append((soff, w))
            soff += w
        iE = (eoff - 128) // 512
        iO = (keys - 128) // 512
        adds = {i: [] for i in range(len(sls))}
        adds[iE].append(("tri", (eoff - 128) - sls[iE][0]))
        adds[iO].append(("oad", None))
        tqk = None
        for kc in range(NCH):
            sl = kc % 2
            key = qt * 16 + kc - 2
            if key >= 0 and key in kst_cons:
                wait("sync", sT, kst_cons[key])
            t_k1 = dma("sync", Kst[sl][:, 0:eoff],
                       ccKout[kc * 128:(kc + 1) * 128, 0:eoff], sKst[sl])
            t_k2 = dma("sync", Kst[sl][:, eoff:keys],
                       ccKout[E + kc * 128:E + (kc + 1) * 128, 0:eoff], sKst[sl])
            wait("tensor", sKst[sl], t_k2)
            if kc == 0 and prev_opv is not None:
                wait("tensor", sS, prev_opv)
            for i, (off, w) in enumerate(sls):
                tqk = mark("tensor", lambda E_, c_=kc, o_=off, w_=w,
                           q_=qt, sl_=sl, op_=(not adds[i]): E_.matmul(
                    out=psS[:, o_:o_ + w_],
                    lhsT=qbuf[:, c_ * 1024 + q_ * 128:c_ * 1024 + (q_ + 1) * 128],
                    rhs=Kst[sl_][:, o_:o_ + w_],
                    start=(c_ == 0),
                    stop=(c_ == NCH - 1 and op_)), sT)
            kst_cons[qt * 16 + kc] = tqk
        for i, (off, w) in enumerate(sls):
            for j, (kind, p_) in enumerate(adds[i]):
                rhs = (CSTB[:, 1408 - p_:1408 - p_ + w] if kind == "tri"
                       else CSTB[:, 2048 - w:2048])
                tqk = mark("tensor", lambda E_, o_=off, w_=w, m=rhs,
                           ls=(j == len(adds[i]) - 1): E_.matmul(
                    out=psS[:, o_:o_ + w_], lhsT=identb_ap,
                    rhs=m, start=False, stop=ls), sT)
        m1_cons[qt] = tqk
        wait("vector", sT, tqk)
        if prev_sh1 is not None:
            wait("vector", sS, prev_sh1[-1])
        tcmb = mark("vector", lambda E_, k_=keys: E_.reduce_max(
            out=negmax1, in_=psS[:, 0:k_], axis=AX.X, negate=True), sV)
        wait("scalar", sV, tcmb)
        if prev_pt1 is not None:
            wait("scalar", sT, prev_pt1)
        prev_sh1 = []
        soff = 0
        while soff < keys:
            w = min(512, keys - soff)
            tshn = mark("scalar", lambda E_, o_=soff, w_=w: E_.activation(
                out=P1[:, o_:o_ + w_], in_=psS[:, o_:o_ + w_],
                func=AF.Identity, bias=negmax1), sS)
            prev_sh1.append(tshn)
            soff += w
        tex = mark("scalar", lambda E_, k_=keys: E_.activation(
            out=P1[:, 0:k_], in_=P1[:, 0:k_], func=AF.Exp,
            accum_out=rowsum1), sS)
        wait("vector", sS, tex)
        trc = mark("vector", lambda E_: E_.reciprocal(out=rinv1, in_=rowsum1), sV)
        if prev_pt1 is not None:
            wait("vector", sT, prev_pt1)
        tdg = mark("vector", lambda E_: E_.tensor_scalar(
            out=diag1, in0=identb_ap, scalar1=rinv1, scalar2=None,
            op0=ALU.mult), sV)
        wait("tensor", sS, prev_sh1[-1])
        emit_pt1(qt, tdg, ntile)
        emit_pv1(qt, ntile)
    for nm in ("Kst0", "Kst1", "Vst0", "Vst1", "PT1", "Otk", "P1",
               "qbuf", "diag1"):
        segb.release(nm)
    segf.release("Sf1")
    ow = [bf("ow0", NCH * 128), bf("ow1", NCH * 128)]
    ow_cons = {}
    ev1_cons = {}
    ev_tok1 = None
    wait("tensor", sS, o1_done)
    wait("gpsimd", sT, prev_pt1)
    wait("gpsimd", sS, o1_done)
    for mt in range(NCH):
        if mt >= 2:
            wait("gpsimd", sT, ow_cons[mt - 2])
        t_w = dma("gpsimd", ow[mt % 2], oprojw[mt], sWo[mt % 2])
        wait("tensor", sWo[mt % 2], t_w)
        if mt == 0:
            wait("tensor", sS, prev_opv)
        else:
            wait("tensor", sV, ev_tok1)
        tmm = None
        o1v = O1.rearrange("p (q x) -> p q x", x=2048)
        for kc in range(NCH):
            for n in range(2):
                tmm = mark("tensor", lambda E, k_=kc, n_=n, m_=mt, v=o1v: E.matmul(
                    out=psA[:, n_ * 512:(n_ + 1) * 512],
                    lhsT=ow[m_ % 2][:, k_ * 128:(k_ + 1) * 128],
                    rhs=v[:, n_ * 4:(n_ + 1) * 4, k_ * 128:(k_ + 1) * 128],
                    start=(k_ == 0), stop=(k_ == NCH - 1)), sT)
        ow_cons[mt] = tmm
        wait("vector", sT, tmm)
        ev_tok1 = mark("vector", lambda E, m_=mt: E.scalar_tensor_tensor(
            out=R[m_], in0=R[m_], scalar=2.0, in1=psA[:],
            op0=ALU.mult, op1=ALU.add), sV)
        ev1_cons[mt] = ev_tok1
    for nm in ("O1", "ow0", "ow1"):
        segb.release(nm)
    segf.release("cols1")
    wait("scalar", sV, ev_tok1)
    lnv, lns = post_ln(c0r(1), R, R)

    if dbg_phase == "C":
        return finish()

    meanB = f32("d_meanB", TOK)
    invB = f32("d_invB", TOK)
    d_scr = f32("d_scr", TOK)
    maxacc = f32("d_max", TOK)
    sB = f32("d_sB", TOK)
    invsB = sB
    tbq = [bf("d_tb", TOK), bf("d_tb2", TOK)]
    sqq = [bf("d_sq", TOK), bf("d_sq2", TOK)]
    wait("scalar", sV, lnv)
    mm_tok = None
    for c in range(NCH):
        tc1 = mark("scalar", lambda E, s=R[c], d=tbq[c % 2]: E.copy(out=d, in_=s), sS)
        tc2 = mark("scalar", lambda E, s=R[c], d=sqq[c % 2]: E.activation(
            out=d, in_=s, func=AF.Square), sS)
        wait("tensor", sS, tc2)
        for n in range(2):
            mm_tok = mark("tensor", lambda E, c_=c, n_=n, t=tbq[c % 2]: E.matmul(
                out=psS[0:1, n_ * 512:(n_ + 1) * 512], lhsT=onesb_ap,
                rhs=t[:, n_ * 512:(n_ + 1) * 512],
                start=(c_ == 0), stop=(c_ == NCH - 1)), sT)
            mm_tok = mark("tensor", lambda E, c_=c, n_=n, q=sqq[c % 2]: E.matmul(
                out=psS[0:1, 1024 + n_ * 512:1024 + (n_ + 1) * 512], lhsT=onesb_ap,
                rhs=q[:, n_ * 512:(n_ + 1) * 512],
                start=(c_ == 0), stop=(c_ == NCH - 1)), sT)
        if c >= 1:
            wait("scalar", sT, mm_tok)
    mean_r = meanB[0:1, 0:TOK]
    var_r = invB[0:1, 0:TOK]
    wait("scalar", sT, mm_tok)
    t1 = mark("scalar", lambda E: E.mul(out=mean_r, in_=psS[0:1, 0:1024],
                                        mul=c0r(0)), sS)
    t1 = mark("scalar", lambda E: E.activation(out=var_r, in_=mean_r,
                                               func=AF.Square), sS)
    wait("vector", sS, t1)
    t2 = mark("vector", lambda E: E.scalar_tensor_tensor(
        out=var_r, in0=psS[0:1, 1024:2048], scalar=c0r(0), in1=var_r,
        op0=ALU.mult, op1=ALU.subtract), sV)
    wait("scalar", sV, t2)
    t3 = mark("scalar", lambda E: E.activation(out=var_r, in_=var_r,
                                               func=AF.Ln, bias=c0r(2)), sS)
    t3 = mark("scalar", lambda E: E.activation(out=var_r, in_=var_r,
                                               func=AF.Exp, scale=c0r(3)), sS)
    wait("tensor", sS, t3)
    tb_tok = None
    for n in range(2):
        tb_tok = mark("tensor", lambda E, n_=n: E.matmul(
            out=psS[:, n_ * 512:(n_ + 1) * 512], lhsT=onesf_ap,
            rhs=mean_r[0:1, n_ * 512:(n_ + 1) * 512], start=True, stop=True), sT)
        tb_tok = mark("tensor", lambda E, n_=n: E.matmul(
            out=psS[:, 1024 + n_ * 512:1024 + (n_ + 1) * 512], lhsT=onesf_ap,
            rhs=var_r[0:1, n_ * 512:(n_ + 1) * 512], start=True, stop=True), sT)
    wait("scalar", sT, tb_tok)
    t4 = mark("scalar", lambda E: E.copy(out=meanB, in_=psS[:, 0:1024]), sS)
    t4 = mark("scalar", lambda E: E.copy(out=invB, in_=psS[:, 1024:2048]), sS)
    minacc = sB
    tmx = None
    for c in range(NCH):
        wait("vector", sS, t4)
        mark("vector", lambda E, s=R[c]: E.tensor_sub(out=d_scr, in0=s, in1=meanB), sV)
        mark("vector", lambda E: E.tensor_mul(out=d_scr, in0=d_scr, in1=invB), sV)
        if c == 0:
            tmx = mark("vector", lambda E: E.tensor_copy(out=maxacc, in_=d_scr), sV)
            tmx = mark("vector", lambda E: E.tensor_copy(out=minacc, in_=d_scr), sV)
        else:
            tmx = mark("vector", lambda E: E.tensor_max(
                out=maxacc, in0=maxacc, in1=d_scr), sV)
            tmx = mark("vector", lambda E: E.tensor_tensor(
                out=minacc, in0=minacc, in1=d_scr, op=ALU.min), sV)
    tmx = mark("vector", lambda E: E.tensor_scalar(
        out=minacc, in0=minacc, scalar1=-1.0, scalar2=None, op0=ALU.mult), sV)
    tmx = mark("vector", lambda E: E.tensor_max(
        out=maxacc, in0=maxacc, in1=minacc), sV)
    mxc = maxacc[:, 0:8]
    srow8 = maxacc[:, 128:256]
    srowf = d_scr
    trt = None
    for g in range(2):
        wait("tensor", sV, tmx)
        if g > 0:
            wait("tensor", sV, trr)
        for j in range(4):
            t_i = g * 4 + j
            trt = mark("tensor", lambda E, t=t_i, j_=j: E.transpose(
                out=psA[:, j_ * 128:(j_ + 1) * 128],
                in_=maxacc[:, t * 128:(t + 1) * 128], identity=identf_ap), sT)
        wait("vector", sT, trt)
        trr = None
        for j in range(4):
            t_i = g * 4 + j
            trr = mark("vector", lambda E, t=t_i, j_=j: E.reduce_max(
                out=mxc[:, t:t + 1], in_=psA[:, j_ * 128:(j_ + 1) * 128],
                axis=AX.X), sV)
    wait("tensor", sV, trr)
    trt = mark("tensor", lambda E: E.transpose(
        out=psA[0:8, 0:128], in_=mxc[:, 0:8], identity=identf_ap), sT)
    wait("scalar", sT, trt)
    t5 = mark("scalar", lambda E: E.copy(out=srow8[0:8, :], in_=psA[0:8, 0:128]), sS)
    wait("sync", sS, t5)
    t5d = dma("sync", srowf[0:1, 0:1024], srow8[0:8, 0:128], sGrel)
    wait("tensor", sGrel, t5d)
    tb_tok = None
    for n in range(2):
        tb_tok = mark("tensor", lambda E, n_=n: E.matmul(
            out=psS[:, n_ * 512:(n_ + 1) * 512], lhsT=onesf_ap,
            rhs=srowf[0:1, n_ * 512:(n_ + 1) * 512], start=True, stop=True), sT)
    wait("vector", sT, tb_tok)
    t6 = mark("vector", lambda E: E.reciprocal(out=sB, in_=psS[:, 0:1024]), sV)
    t6 = mark("vector", lambda E: E.tensor_scalar(
        out=sB, in0=sB, scalar1=127.0, scalar2=None, op0=ALU.mult), sV)
    if dbg_phase == "D0":
        wait("sync", sS, sS.n)
        wait("sync", sV, sV.n)
        tdone = dma("sync", out[0:128, :], sB, sO)
        tdone = dma("sync", out[128:256, :], maxacc, sO)
        tdone = dma("sync", out[256:384, :], meanB, sO)
        tdone = dma("sync", out[384:512, :], invB, sO)
        tdone = dma("sync", out[512:640, :], d_scr, sO)
        wait("sync", sO, tdone)
        p_finalize(nc, p, ctx)
        return nc
    xq = [bf(f"xq{c}", TOK) for c in range(NCH)]
    i8t = AI8[:, 0:1024]
    xq_tok = None
    for c in range(NCH):
        mark("vector", lambda E, s=R[c]: E.tensor_sub(out=d_scr, in0=s, in1=meanB), sV)
        mark("vector", lambda E: E.tensor_mul(out=d_scr, in0=d_scr, in1=invB), sV)
        tvq = mark("vector", lambda E: E.tensor_mul(out=d_scr, in0=d_scr, in1=sB), sV)
        wait("vector", sS, xq_tok) if xq_tok else None
        tvq = mark("vector", lambda E: E.tensor_copy(out=i8t, in_=d_scr), sV)
        wait("scalar", sV, tvq)
        xq_tok = mark("scalar", lambda E, c_=c: E.copy(out=xq[c_], in_=i8t), sS)
    if dbg_phase == "D1":
        tdone = None
        wait("gpsimd", sS, sS.n)
        wait("gpsimd", sV, sV.n)
        for c in range(NCH):
            tdone = dma("gpsimd", out[c * 128:(c + 1) * 128, :], xq[c], sO)
        wait("gpsimd", sO, tdone)
        p_finalize(nc, p, ctx)
        return nc
    t6 = mark("vector", lambda E: E.reciprocal(out=sB, in_=sB), sV)
    gwb = [bf("gwb0", S), bf("gwb1", S)]
    uwb = [bf("uwb0", S), bf("uwb1", S)]
    fz1b = [bf("fz1b0", TOK), bf("fz1b1", TOK)]
    gw_cons = {}
    uw_cons = {}
    fz1_cons = {}
    prev_g1 = None
    prev_u1 = None
    sDg = [dsem("sDg0"), dsem("sDg1")]
    sDu = [dsem("sDu0"), dsem("sDu1")]
    wait("gpsimd", sS, xq_tok)
    for ic in range(64):
        sl = ic % 2
        if ic >= 2:
            wait("gpsimd", sT, gw_cons[ic - 2])
        t_g = dma("gpsimd", gwb[sl], wg1[ic], sDg[sl])
        wait("scalar", sDg[sl], t_g)
        sg = mark("scalar", lambda E, sl_=sl: E.sign(out=gwb[sl_], in_=gwb[sl_]), sS)
        if ic >= 2:
            wait("gpsimd", sT, uw_cons[ic - 2])
        t_u = dma("gpsimd", uwb[sl], wu1[ic], sDu[sl])
        wait("scalar", sDu[sl], t_u)
        su = mark("scalar", lambda E, sl_=sl: E.sign(out=uwb[sl_], in_=uwb[sl_]), sS)
        wait("tensor", sS, sg)
        wait("tensor", sS, xq_tok)
        if prev_g1 is not None:
            wait("tensor", sV, prev_g1)
        tg = None
        for c in range(NCH):
            for n in range(2):
                tg = mark("tensor", lambda E, c_=c, n_=n, sl_=sl: E.matmul(
                    out=psA[:, n_ * 512:(n_ + 1) * 512],
                    lhsT=gwb[sl_][:, c_ * 128:(c_ + 1) * 128],
                    rhs=xq[c_][:, n_ * 512:(n_ + 1) * 512],
                    start=(c_ == 0), stop=(c_ == NCH - 1)), sT)
        gw_cons[ic] = tg
        wait("tensor", sS, su)
        if prev_u1 is not None:
            wait("tensor", sV, prev_u1)
        tu = None
        for c in range(NCH):
            for n in range(2):
                tu = mark("tensor", lambda E, c_=c, n_=n, sl_=sl: E.matmul(
                    out=psS[:, n_ * 512:(n_ + 1) * 512],
                    lhsT=uwb[sl_][:, c_ * 128:(c_ + 1) * 128],
                    rhs=xq[c_][:, n_ * 512:(n_ + 1) * 512],
                    start=(c_ == 0), stop=(c_ == NCH - 1)), sT)
        uw_cons[ic] = tu
        wait("vector", sT, tg)
        prev_g1 = mark("vector", lambda E: E.scalar_tensor_tensor(
            out=d_scr, in0=psA[:], scalar=1.0, in1=invsB,
            op0=ALU.mult, op1=ALU.mult), sV)
        wait("scalar", sV, prev_g1)
        sgs = mark("scalar", lambda E: E.activation(out=d_scr, in_=d_scr,
                                                    func=AF.Silu), sS)
        wait("vector", sT, tu)
        wait("vector", sS, sgs)
        prev_u1 = mark("vector", lambda E: E.scalar_tensor_tensor(
            out=maxacc, in0=psS[:, 0:1024], scalar=1.0, in1=invsB,
            op0=ALU.mult, op1=ALU.mult), sV)
        if ic >= 2:
            wait("vector", sFz[sl], fz1_cons[ic - 2])
        tf = mark("vector", lambda E, sl_=sl: E.tensor_mul(
            out=fz1b[sl_], in0=d_scr, in1=maxacc), sV)
        wait("sync", sV, tf)
        fz1_cons[ic] = dma("sync", fuse1[ic * 128:(ic + 1) * 128, :],
                           fz1b[sl], sFz[sl])
    dwb1 = [bf("dw1b0", 2 * 512), bf("dw1b1", 2 * 512)]
    fr1b = [bf("fr1b0", TOK), bf("fr1b1", TOK)]
    ev_d = None
    fr_cons1 = {}
    dw_cons1 = {}
    wait("sync", sFz[0], fz1_cons[62])
    wait("sync", sFz[1], fz1_cons[63])
    pairs1 = [(g, k) for g in range(4) for k in range(32)]

    def stage_dn1(idx):
        grp, kb = pairs1[idx]
        sl = kb % 2
        if idx >= 2 and pairs1[idx - 2] in dw_cons1:
            wait("gpsimd", sT, dw_cons1[pairs1[idx - 2]])
        t_d = dma("gpsimd", dwb1[sl], wd2[grp, kb], sDw[sl])
        wait("scalar", sDw[sl], t_d)
        return mark("scalar", lambda E, sl_=sl: E.sign(
            out=dwb1[sl_], in_=dwb1[sl_]), sS)

    staged_dn1 = stage_dn1(0)
    dn_tgt = [[psS[:, 0:512], psS[:, 512:1024]],
              [psS[:, 1024:1536], psS[:, 1536:2048]],
              [psA[:, 0:512], psA[:, 512:1024]],
              [psX6[:], psO[:]]]
    for idx, (grp, kb) in enumerate(pairs1):
        if kb == 0:
            mts = [4 * grp, 4 * grp + 1, 4 * grp + 2, 4 * grp + 3]
        sl = kb % 2
        s8d = staged_dn1
        tt = None
        for j in range(2):
            kc = 2 * kb + j
            if kc >= 2 and (grp, kc - 2) in fr_cons1:
                wait("sync", sT, fr_cons1[(grp, kc - 2)])
            elif kc < 2 and grp > 0 and (grp - 1, 62 + kc) in fr_cons1:
                wait("sync", sT, fr_cons1[(grp - 1, 62 + kc)])
            t_f = dma("sync", fr1b[kc % 2],
                      fuse1[kc * 128:(kc + 1) * 128, :], sFr[kc % 2])
            wait("tensor", sS, s8d)
            wait("tensor", sFr[kc % 2], t_f)
            if kc == 0 and ev_d is not None:
                wait("tensor", sV, ev_d)
            for mi, m in enumerate(mts):
                for n in range(2):
                    tt = mark("tensor", lambda E, mi_=mi, j_=j, n_=n,
                              k_=kc, sl_=sl: E.matmul(
                        out=dn_tgt[mi_][n_],
                        lhsT=dwb1[sl_][:, j_ * 512 + mi_ * 128:
                                       j_ * 512 + (mi_ + 1) * 128],
                        rhs=fr1b[k_ % 2][:, n_ * 512:(n_ + 1) * 512],
                        start=(k_ == 0), stop=(k_ == 63)), sT)
            fr_cons1[(grp, kc)] = tt
        dw_cons1[(grp, kb)] = tt
        if idx + 1 < len(pairs1):
            staged_dn1 = stage_dn1(idx + 1)
        if kb == 31:
            wait("vector", sT, dw_cons1[(grp, 31)])
            for mi, m in enumerate(mts):
                for n in range(2):
                    ev_d = mark("vector", lambda E, mi_=mi, m_=m, n_=n:
                                E.scalar_tensor_tensor(
                        out=R[m_][:, n_ * 512:(n_ + 1) * 512],
                        in0=dn_tgt[mi_][n_], scalar=1.0,
                        in1=R[m_][:, n_ * 512:(n_ + 1) * 512],
                        op0=ALU.mult, op1=ALU.add), sV)
    for nm in ("d_meanB", "d_invB", "d_scr", "d_max", "d_sB"):
        segf.release(nm)
    wait("scalar", sV, ev_d)
    lnv, lns = post_ln(c0r(1), R, R)
    return finish()


def p_finalize(nc, p, ctx):
    with nc.Block() as block:
        p.run(block)
    ctx.close()


def _stage_inputs(L0_attn_x, L0_attn_Wq, L0_attn_Wkv, L0_attn_Wkvh, L0_attn_Wg,
                  L0_attn_Wout, L0_ffn_gate_w, L0_ffn_up_w, L0_ffn_down_w,
                  L0_ffn_gate_scale, L0_ffn_up_scale, L0_ffn_down_scale,
                  L1_attn_qkv, L1_attn_o_proj, L1_attn_gamma, L1_attn_beta,
                  L1_ffn_W_g, L1_ffn_W_u, L1_ffn_W_d):
    f32c = lambda a: np.ascontiguousarray(a, dtype=np.float32)
    wq_st = rearr_lhsT(f32c(L0_attn_Wq.T))
    wkv_st = rearr_lhsT(f32c(L0_attn_Wkv.T))
    kvg = np.concatenate([f32c(L0_attn_Wkvh), f32c(L0_attn_Wg)], axis=0)
    wkvg_st = rearr_lhsT(kvg.T, j=64)[0]
    wout_st = rearr_lhsT(f32c(L0_attn_Wout.T))
    gatew_st = rearr_lhsT(f32c(L0_ffn_gate_w))
    upw_st = rearr_lhsT(f32c(L0_ffn_up_w))
    def group_down(wmat, nkb):
        a = f32c(wmat).reshape(nkb, 2, 128, 4, 512).transpose(3, 0, 2, 1, 4)
        return np.ascontiguousarray(a.reshape(4, nkb, 128, 1024))
    downw2_st = group_down(L0_ffn_down_w, 24)
    qkvw_st = rearr_lhsT(f32c(L1_attn_qkv[:, 0:2 * E]))
    vweight_st = np.ascontiguousarray(
        f32c(L1_attn_qkv[:, 2 * E:3 * E]).reshape(NCH, 128, E))
    oprojw_st = rearr_lhsT(f32c(L1_attn_o_proj))
    wg1_st = rearr_lhsT(f32c(L1_ffn_W_g.T))
    wu1_st = rearr_lhsT(f32c(L1_ffn_W_u.T))
    wd2_st = group_down(np.ascontiguousarray(f32c(L1_ffn_W_d).T), 32)

    colmaj = lambda v, n: np.ascontiguousarray(
        f32c(v).reshape(n, 128).T)
    gsc_st = colmaj(L0_ffn_gate_scale, 48)
    usc_st = colmaj(L0_ffn_up_scale, 48)
    dsc_st = colmaj(L0_ffn_down_scale, 16)
    gb_st = np.concatenate([colmaj(L1_attn_gamma, 16), colmaj(L1_attn_beta, 16)],
                           axis=1)

    pos = np.arange(S, dtype=np.float64)
    inv = 10000.0 ** (-np.arange(0, D, 2, dtype=np.float64) / D)
    ang = pos[None, :] * inv[:, None]
    cosf = np.cos(ang)
    sinf = np.sin(ang)
    cosQ_full = np.concatenate([cosf, cosf], axis=0).astype(np.float32)
    sinQ_full = np.concatenate([-sinf, sinf], axis=0).astype(np.float32)
    sc = 1.0 / np.sqrt(D)
    cosK_full = (np.concatenate([cosf, cosf], axis=0) * sc).astype(np.float32)
    sinK_full = (np.concatenate([-sinf, sinf], axis=0) * sc).astype(np.float32)

    identb_st = np.eye(128, dtype=ml_dtypes.bfloat16)
    identf_st = np.eye(128, dtype=np.float32)
    onesb_st = np.ones((128, 1), dtype=ml_dtypes.bfloat16)
    onesf_st = np.ones((1, 128), dtype=np.float32)
    consts_st = np.tile(np.array(
        [[1.0 / 2048, 1e-5, 1e-8, -0.5, 0.0, 1.0 / 127, 127.0,
          1.0 / np.sqrt(2048.0)]], np.float32), (128, 1))

    shared = dict(
        wq=wq_st, wkv=wkv_st, wkvg=wkvg_st, wout=wout_st, gatew=gatew_st,
        upw=upw_st, downw2=downw2_st, gsc=gsc_st, usc=usc_st, dsc=dsc_st,
        qkvw=qkvw_st, vweight=vweight_st, oprojw=oprojw_st, gammab=gb_st,
        wg1=wg1_st, wu1=wu1_st, wd2=wd2_st, identb=identb_st,
        identf=identf_st, onesb=onesb_st, onesf=onesf_st, consts=consts_st)

    in_maps = []
    x = f32c(L0_attn_x)
    for c in range(8):
        b, half = c // 2, c % 2
        own = slice(half * TOK, (half + 1) * TOK)
        xb_ = x[b].T
        x0T_st = np.ascontiguousarray(xb_)
        qpos = (np.arange(8)[:, None] * 256 + half * 128 +
                np.arange(128)[None, :]).ravel()
        x0To_st = np.ascontiguousarray(xb_[:, qpos])
        cosQ_st = np.ascontiguousarray(cosQ_full[:, qpos])
        sinQ_st = np.ascontiguousarray(sinQ_full[:, qpos])
        cosK_st = np.ascontiguousarray(cosK_full)
        sinK_st = np.ascontiguousarray(sinK_full)
        p_ = np.arange(128)
        tri = np.where(p_[None, :] <= p_[:, None], 0.0, NEG)
        z128 = np.zeros((128, 128))
        neg128 = np.full((128, 128), NEG)
        if half == 0:
            mb = np.concatenate([tri, neg128], axis=1)
            etail, otail = tri, neg128
        else:
            mb = np.concatenate([z128, tri], axis=1)
            etail, otail = z128, tri
        z256 = np.zeros((128, 256))
        z384 = np.zeros((128, 384))
        maskb_st = np.ascontiguousarray(np.concatenate(
            [mb, z256, mb,
             z384, etail,
             z384, otail], axis=1).astype(ml_dtypes.bfloat16))
        in_maps.append(dict(shared, x0T=x0T_st, x0To=x0To_st, cosQ=cosQ_st,
                            sinQ=sinQ_st, cosK=cosK_st, sinK=sinK_st,
                            maskb=maskb_st))
    return in_maps


def kernel(**inputs):
    in_maps = _stage_inputs(**inputs)
    nc = build_nc()
    res = run_bass_kernel_spmd(nc, in_maps, core_ids=list(range(8)))
    outp = np.empty((B, S, E), np.float32)
    qpos_h = [(np.arange(8)[:, None] * 256 + h_ * 128 +
               np.arange(128)[None, :]).ravel() for h_ in range(2)]
    for c in range(8):
        b, half = c // 2, c % 2
        o = np.asarray(res.results[c]["out"])
        outp[b, qpos_h[half], :] = o.T
    return outp
